# version 44
# baseline (speedup 1.0000x reference)
import contextlib
import numpy as np
import ml_dtypes

import concourse.bass as bass
import concourse.mybir as mybir
from concourse.bass_utils import run_bass_kernel_spmd
from concourse.bass_types import AP

F32 = mybir.dt.float32
BF16 = mybir.dt.bfloat16
AF = mybir.ActivationFunctionType
ALU = mybir.AluOpType
AX = mybir.AxisListType

D = 1024
H = 8
DH = 64
KC = 8
NEGBIG = -3.0e38
EPS_LN = 1e-5
EPS_RMS = float(np.finfo(np.float32).eps)
BF = ml_dtypes.bfloat16


class Op:
    __slots__ = ("eng", "emit", "deps", "dma", "need_inc", "inc_val", "waits", "tag", "sem_key", "idx", "drain_before", "chain")

    def __init__(self, eng, emit, deps, dma, tag):
        self.eng = eng
        self.emit = emit
        self.deps = deps
        self.dma = dma
        self.need_inc = dma
        self.inc_val = 0
        self.waits = None
        self.tag = tag
        self.drain_before = False
        self.chain = None


class Res:
    __slots__ = ("writers", "readers")

    def __init__(self):
        self.writers = []
        self.readers = []

    def rdeps(self):
        return [(w, True) for w in self.writers]

    def wdeps(self):
        return [(r, True) for r in self.readers] + [(w, False) for w in self.writers]

    def note_read(self, op):
        self.readers.append(op)

    def note_write(self, op):
        if self.readers:
            self.writers = [op]
            self.readers = []
        else:
            self.writers.append(op)


class Prog:
    ENGS = ("T", "V", "S", "G", "Q")

    def __init__(self, race_safe=False):
        self.ops = {e: [] for e in self.ENGS}
        self.fence_deps = []
        self.fence_exclude = ()
        self.auto_deps = {}
        self.race_safe = race_safe

    def do(self, eng, emit, reads=(), writes=(), dma=False, tag="", chain=None):
        deps = [] if eng in self.fence_exclude else \
            [(d, True) for d in self.fence_deps]
        for r in self.auto_deps.get(eng, ()):
            deps += r.rdeps()
        for r in reads:
            deps += r.rdeps()
        for w in writes:
            deps += w.wdeps()
        op = Op(eng, emit, deps, dma, tag)
        op.chain = chain
        op.idx = len(self.ops[eng])
        self.ops[eng].append(op)
        for r in reads:
            r.note_read(op)
        for w in writes:
            w.note_write(op)
        return op

    def fence(self, exclude=()):
        self.fence_deps = [self.ops[e][-1] for e in self.ENGS if self.ops[e]]
        self.fence_exclude = exclude


    NCHAIN = 4

    def finalize(self):
        for e in self.ENGS:
            for op in self.ops[e]:
                for d, _hz in op.deps:
                    if d.eng != op.eng or d.dma:
                        d.need_inc = True
        totals = {}
        for e in self.ENGS:
            if e in ("G", "Q"):
                nd = 0
                chain_vals = [0] * self.NCHAIN
                for op in self.ops[e]:
                    if op.chain is not None:
                        c = op.chain % self.NCHAIN
                    else:
                        c = nd % self.NCHAIN
                        nd += 1
                    chain_vals[c] += 16
                    op.sem_key = f"{e}{c}"
                    op.inc_val = chain_vals[c]
                for c in range(self.NCHAIN):
                    totals[f"{e}{c}"] = chain_vals[c]
            else:
                cnt = 0
                for op in self.ops[e]:
                    if op.need_inc:
                        cnt += 1
                    op.sem_key = e
                    op.inc_val = cnt
                totals[e] = cnt
        for e in self.ENGS:
            hwm = {}
            for op in self.ops[e]:
                w = {}
                if op.dma:
                    if op.inc_val > 16:
                        w[op.sem_key] = op.inc_val - 16
                for dd, _hz in op.deps:
                    if dd.eng == op.eng and not dd.dma:
                        continue
                    if dd.inc_val > w.get(dd.sem_key, 0):
                        w[dd.sem_key] = dd.inc_val
                op.waits = [(pe, v) for pe, v in w.items() if v > hwm.get(pe, 0)]
                for pe, v in op.waits:
                    hwm[pe] = v
        drain_engines = ("V", "S") if self.race_safe else ("V",)
        for e in self.ENGS:
            if e not in drain_engines:
                continue
            drained_to = -1
            for op in self.ops[e]:
                need = -1
                for dd, hz in op.deps:
                    if (hz or self.race_safe) and dd.eng == e and not dd.dma \
                            and dd.idx > need:
                        need = dd.idx
                if need > drained_to:
                    op.drain_before = True
                    drained_to = op.idx - 1
                else:
                    op.drain_before = False
        return totals


def build_program(N, race_safe=False):
    T = N // 128
    IC = N // 512
    assert IC * 512 == N and T % 4 == 0

    nc = bass.Bass("TRN2", target_bir_lowering=False, debug=False,
                   detect_race_conditions=race_safe)

    x_h = nc.dram_tensor("x", [N, D], F32, kind="ExternalInput").ap()
    wq_h = nc.dram_tensor("wq", [H, D, DH], F32, kind="ExternalInput").ap()
    wk_h = nc.dram_tensor("wk", [H, D, DH], F32, kind="ExternalInput").ap()
    wv_h = nc.dram_tensor("wv", [H, D, DH], F32, kind="ExternalInput").ap()
    wo_h = nc.dram_tensor("wo", [H, DH, D], F32, kind="ExternalInput").ap()
    wg_h = nc.dram_tensor("wg", [D, H], F32, kind="ExternalInput").ap()
    rmsw_h = nc.dram_tensor("rmsw", [128, KC], F32, kind="ExternalInput").ap()
    wf_h = nc.dram_tensor("wf", [D, D], F32, kind="ExternalInput").ap()
    identbf_h = nc.dram_tensor("identbf", [128, 128], BF16, kind="ExternalInput").ap()
    identf_h = nc.dram_tensor("identf", [128, 128], F32, kind="ExternalInput").ap()
    umask_h = nc.dram_tensor("umask", [128, 128], BF16, kind="ExternalInput").ap()
    ubin_h = nc.dram_tensor("ubin", [128, 128], BF16, kind="ExternalInput").ap()
    out_h = nc.dram_tensor("out", [N, D], F32, kind="ExternalOutput").ap()

    p = Prog(race_safe)
    stack = contextlib.ExitStack()

    def alloc(st, name, shape, dtype, side="left"):
        return st.enter_context(nc.sbuf_tensor(name, shape, dtype, side=side))

    X = alloc(stack, "X", [128, T, D], F32)
    identbf = alloc(stack, "identbf_s", [128, 128], BF16)
    identf = alloc(stack, "identf_s", [128, 128], F32)
    umask = alloc(stack, "umask_s", [128, 128], BF16)
    ubin = alloc(stack, "ubin_s", [128, 128], BF16)
    stat = alloc(stack, "stat", [128, 24 * T], F32)
    rmswc = alloc(stack, "rmswc", [128, KC], F32)
    sq_scr = alloc(stack, "sq_scr", [128, D], BF16)
    ones1 = alloc(stack, "ones1", [1, 128], BF16)
    onesc = alloc(stack, "onesc", [65, 1], BF16)
    CONST_VALS = [EPS_LN, EPS_RMS, 0.125, 0.5, -1.0, 1.0 / D, 1.0, 0.0, 1e-30]
    cdb = alloc(stack, "cdb", [128, len(CONST_VALS)], F32)
    for ci, cv in enumerate(CONST_VALS):
        nc.const_aps.register(F32, cv, cdb[:, ci:ci + 1])
    epsr = alloc(stack, "epsr", [128, 1], F32)

    ps8 = stack.enter_context(nc.psum_tensor("ps8", [128, 3072], F32))
    psbf = stack.enter_context(nc.psum_tensor("psbf", [128, 2048], BF16))

    rX = [Res() for _ in range(T)]
    rPS = [Res() for _ in range(6)]
    rPB = [Res() for _ in range(2)]
    rStatK = {}

    def rSt(base, half):
        return rStatK.setdefault((base, half), Res())
    rStSum = Res()
    rStSq = Res()
    rQhat = [Res() for _ in range(H)]
    rKhat = [Res() for _ in range(H)]
    rV = [Res() for _ in range(T)]
    rGtok = Res()
    rLscr = Res()
    rGscr = Res()
    rRmsw = Res()
    rConst = Res()
    rSq = Res()

    def bank(b):
        return ps8[:, b * 512:(b + 1) * 512]

    def banks(b0, nb):
        return ps8[:, b0 * 512:(b0 + nb) * 512]

    for dst, src in ((identbf, identbf_h), (identf, identf_h), (umask, umask_h),
                     (ubin, ubin_h)):
        p.do("Q", (lambda dst=dst, src=src: lambda q: q.dma_start(
            out=dst[:, :], in_=src[:, :]))(), writes=[rConst], dma=True, tag="ldc")
    xstep = max(2, T // 8)
    for tt in range(0, T, xstep):
        p.do("Q", (lambda tt=tt, xstep=xstep: lambda q: q.dma_start(
            out=X[:, tt:tt + xstep, :],
            in_=x_h.rearrange("(t pp) d -> pp t d", pp=128)[:, tt:tt + xstep, :]))(),
            writes=[rX[tt + i] for i in range(xstep)], dma=True, tag="ldX")
    p.do("Q", lambda q: q.dma_start(
        out=rmswc[:, :], in_=rmsw_h[:, :]),
        writes=[rRmsw], dma=True, tag="ldrmsw")
    rCAp = Res()
    for ci, cv in enumerate(CONST_VALS):
        p.do("V", (lambda ci=ci, cv=cv: lambda v: v.memset(cdb[:, ci:ci + 1], cv))(),
             writes=[rCAp], tag="cdb")
    p.auto_deps = {"S": [rCAp], "V": [rCAp], "T": [rCAp]}
    p.do("V", lambda v: v.memset(ones1[:, :], 1.0), writes=[rConst], tag="ones1")
    p.do("V", lambda v: v.memset(onesc[:, :], 1.0), writes=[rConst], tag="onesc")

    rEps = Res()
    one_c = cdb[:, 6:7]
    p.do("V", lambda v: v.memset(epsr[:, :], 3.0), writes=[rEps], tag="eps0")
    p.do("V", lambda v: v.reciprocal(epsr[:, :], epsr[:, :]),
         reads=[rEps], writes=[rEps], tag="eps1")
    p.do("V", lambda v: v.tensor_scalar_mul(epsr[:, :], epsr[:, :], 4.0),
         reads=[rEps], writes=[rEps], tag="eps2")
    p.do("V", lambda v: v.scalar_tensor_tensor(
        epsr[:, :], one_c, -1.0, epsr[:, :], op0=ALU.mult, op1=ALU.add),
        reads=[rEps], writes=[rEps], tag="eps3")
    p.do("V", lambda v: v.tensor_scalar_mul(epsr[:, :], epsr[:, :], 3.0),
         reads=[rEps], writes=[rEps], tag="eps4")
    p.do("V", lambda v: v.scalar_tensor_tensor(
        epsr[:, :], one_c, -1.0, epsr[:, :], op0=ALU.mult, op1=ALU.add),
        reads=[rEps], writes=[rEps], tag="eps5")
    p.do("V", lambda v: v.tensor_mul(epsr[:, :], epsr[:, :], epsr[:, :]),
        reads=[rEps], writes=[rEps], tag="eps6")
    p.do("S", lambda s: s.activation(epsr[:, :], epsr[:, :], AF.Sqrt),
         reads=[rEps], writes=[rEps], tag="eps7")

    def build_wft(wft, rWft, wld, rWld, wsg, rWsg, tagp):
        for oc in range(KC):
            wb = oc % 2
            p.do("G", (lambda oc=oc, wb=wb: lambda g: g.dma_start(
                out=wld[:, wb, :], in_=wf_h[oc * 128:(oc + 1) * 128, :]))(),
                writes=[rWld[wb]], dma=True, tag="ldW" + tagp)
            p.do("S", (lambda wb=wb: lambda s: s.sign(wsg[:, :], wld[:, wb, :]))(),
                 reads=[rWld[wb]], writes=[rWsg], tag="sgnW" + tagp)
            pb = oc % 2
            for k in range(KC):
                p.do("T", (lambda oc=oc, k=k, pb=pb: lambda t: t.transpose(
                    psbf[:, pb * 1024 + k * 128: pb * 1024 + (k + 1) * 128],
                    wsg[:, k * 128:(k + 1) * 128], identbf[:, :]))(),
                    reads=[rWsg, rConst], writes=[rPB[pb]], tag="trW" + tagp)
            p.do("V", (lambda oc=oc, pb=pb: lambda v: v.tensor_copy(
                wft[:, :, oc * 128:(oc + 1) * 128], psbf[:, pb * 1024:(pb + 1) * 1024]
                .rearrange("pp (k s) -> pp k s", k=KC)))(),
                reads=[rPB[pb]], writes=[rWft[oc // 4]], tag="evW" + tagp)

    def stats_tile(base, tt):
        p.do("V", (lambda base=base, tt=tt: lambda v: v.reduce_sum(
            stat[:, base + 4 * T + tt: base + 4 * T + tt + 1], X[:, tt, :],
            axis=AX.X))(),
            reads=[rX[tt]], writes=[rStSum], tag="lnsum")
        p.do("S", (lambda base=base, tt=tt: lambda s: s.activation(
            sq_scr[:, :], X[:, tt, :], AF.Square,
            accum_out=stat[:, base + 5 * T + tt: base + 5 * T + tt + 1]))(),
            reads=[rX[tt]], writes=[rStSq, rSq], tag="lnsq")

    def stats_math(base, rms=False, lo=0, hi=None):
        if hi is None:
            hi = T
        rStat = rSt(base, 0 if lo == 0 else 1)
        ST = lambda j: stat[:, base + j * T + lo: base + j * T + hi]
        p.do("V", lambda v: v.tensor_scalar_mul(ST(0), ST(4), 1.0 / D),
             reads=[rStSum], writes=[rStat], tag="ln_m")
        p.do("V", lambda v: v.tensor_scalar_mul(ST(5), ST(5), 1.0 / D),
             reads=[rStSq], writes=[rStat], tag="ln_e2")
        p.do("V", lambda v: v.scalar_tensor_tensor(
            ST(6), ST(0), -1.0, ST(0), op0=ALU.mult, op1=ALU.mult),
            reads=[rStat], writes=[rStat], tag="ln_mm")
        p.do("V", lambda v: v.tensor_add(ST(6), ST(6), ST(5)),
             reads=[rStat], writes=[rStat], tag="ln_var")
        p.do("S", lambda s: s.activation(ST(7), ST(6), AF.Sqrt, bias=EPS_LN),
             reads=[rStat], writes=[rStat], tag="ln_sd")
        p.do("V", lambda v: v.reciprocal(ST(1), ST(7)),
             reads=[rStat], writes=[rStat], tag="ln_r")
        if rms:
            p.do("S", lambda s: s.activation(ST(7), ST(6), AF.Copy, bias=EPS_LN),
                 reads=[rStat], writes=[rStat], tag="rms_vpe")
            p.do("V", lambda v: v.reciprocal(ST(7), ST(7)),
                 reads=[rStat], writes=[rStat], tag="rms_rec")
            p.do("V", lambda v: v.tensor_mul(ST(7), ST(6), ST(7)),
                 reads=[rStat], writes=[rStat], tag="rms_E")
            p.do("S", lambda s: s.activation(ST(7), ST(7), AF.Sqrt, bias=epsr[:, 0:1]),
                 reads=[rStat, rEps], writes=[rStat], tag="rms_sq")
            p.do("V", lambda v: v.reciprocal(ST(7), ST(7)),
                 reads=[rStat], writes=[rStat], tag="rms_rec2")
            p.do("V", lambda v: v.tensor_mul(ST(2), ST(1), ST(7)),
                 reads=[rStat], writes=[rStat], tag="rms_rp")
            p.do("V", lambda v: v.scalar_tensor_tensor(
                ST(3), ST(0), -1.0, ST(2), op0=ALU.mult, op1=ALU.mult),
                reads=[rStat], writes=[rStat], tag="ln_nmr")
        else:
            p.do("V", lambda v: v.scalar_tensor_tensor(
                ST(3), ST(0), -1.0, ST(1), op0=ALU.mult, op1=ALU.mult),
                reads=[rStat], writes=[rStat], tag="ln_nmr")

    def ln_stats(base, rms=False):
        for tt in range(T):
            stats_tile(base, tt)
            if tt == T // 2 - 1:
                stats_math(base, rms, 0, T // 2)
        stats_math(base, rms, T // 2, T)

    def stat_col(base, j, tt):
        return stat[:, base + j * T + tt: base + j * T + tt + 1]

    def ffn(base, wft, rWft, lnt, rLnt, lnTt, rLnT, post_tile=None):
        def prep(tt):
            bb = tt % 2
            p.do("S", (lambda tt=tt, bb=bb: lambda s: s.activation(
                lnt[:, bb, :], X[:, tt, :], AF.Identity,
                bias=stat_col(base, 3, tt), scale=stat_col(base, 1, tt)))(),
                reads=[rX[tt], rSt(base, tt // (T // 2))],
                writes=[rLnt[bb]], tag="f_ln")
            for k in range(KC):
                p.do("T", (lambda bb=bb, k=k: lambda t: t.transpose(
                    psbf[:, bb * 1024 + k * 128: bb * 1024 + (k + 1) * 128],
                    lnt[:, bb, k * 128:(k + 1) * 128], identbf[:, :]))(),
                    reads=[rLnt[bb], rConst], writes=[rPB[bb]], tag="f_tr")
            p.do("S", (lambda bb=bb: lambda s: s.activation(
                lnTt[:, bb, :, :], psbf[:, bb * 1024:(bb + 1) * 1024]
                .rearrange("pp (k s) -> pp k s", k=KC), AF.Copy))(),
                reads=[rPB[bb]], writes=[rLnT[bb]], tag="f_ev")

        def mms(tt):
            bb = tt % 2
            yb = bb * 2
            for k in range(KC):
                for hh in range(2):
                    p.do("T", (lambda bb=bb, k=k, hh=hh, yb=yb: lambda t: t.matmul(
                        bank(yb + hh)[:, :], lnTt[:, bb, k, :],
                        wft[:, k, hh * 512:(hh + 1) * 512],
                        start=(k == 0), stop=(k == KC - 1)))(),
                        reads=[rLnT[bb], rWft[hh]], writes=[rPS[yb + hh]], tag="f_mm")
            for hh in range(2):
                p.do("V", (lambda tt=tt, hh=hh, yb=yb: lambda v: v.scalar_tensor_tensor(
                    X[:, tt, hh * 512:(hh + 1) * 512], bank(yb + hh)[:, :], 0.5,
                    X[:, tt, hh * 512:(hh + 1) * 512], op0=ALU.mult, op1=ALU.add))(),
                    reads=[rPS[yb + hh], rX[tt]], writes=[rX[tt]], tag="f_res")

        for tt in range(T):
            prep(tt)
            if tt >= 1:
                mms(tt - 1)
            if post_tile and tt >= 2:
                post_tile(tt - 2)
        mms(T - 1)
        if post_tile:
            post_tile(T - 2)
            post_tile(T - 1)

    lnt = alloc(stack, "lnt", [128, 4, D], BF16, side="right")
    rLnt = [Res(), Res(), Res(), Res()]
    aw_stack = contextlib.ExitStack()
    wq_sb = alloc(aw_stack, "wq_sb", [128, KC, H * DH], BF16, side="right")
    wk_sb = alloc(aw_stack, "wk_sb", [128, KC, H * DH], BF16, side="right")
    wvg_sb = alloc(aw_stack, "wvg_sb", [128, KC, H * DH + H], BF16, side="right")
    wfl = alloc(aw_stack, "wfl", [128, 4, H * DH], F32, side="right")
    rWq, rWk, rWvg = Res(), Res(), Res()
    rWfl = [Res() for _ in range(4)]

    f1_stack = contextlib.ExitStack()
    wft = alloc(f1_stack, "wft1", [128, KC, D], BF16, side="right")
    lnTt = alloc(f1_stack, "lnT1", [128, 2, KC, 128], BF16, side="right")
    wld = alloc(f1_stack, "wld1", [128, 2, D], F32, side="right")
    wsg = alloc(f1_stack, "wsg1", [128, D], BF16, side="right")
    rWft, rWld, rWsg, rLnT = [Res(), Res()], [Res(), Res()], Res(), [Res(), Res()]
    wftsp_h = nc.dram_tensor("wft_sp", [128, KC * D], BF16, kind="Internal").ap()

    build_wft(wft, rWft, wld, rWld, wsg, rWsg, "1")
    ln_stats(0)
    nwl = [0]
    for k in range(KC):
        for dst, src, rdst in ((wq_sb, wq_h, rWq), (wk_sb, wk_h, rWk), (wvg_sb, wv_h, rWvg)):
            fb = nwl[0] % 4
            nwl[0] += 1
            p.do("Q", (lambda k=k, src=src, fb=fb: lambda g: g.dma_start(
                out=wfl[:, fb, :].rearrange("pp (h e) -> pp h e", h=H),
                in_=src[:, k * 128:(k + 1) * 128, :].rearrange("h d e -> d h e")))(),
                writes=[rWfl[fb]], dma=True, tag="ldw_a")
            p.do("V", (lambda k=k, dst=dst, fb=fb: lambda v: v.tensor_scalar(
                dst[:, k, 0:H * DH], wfl[:, fb, :], rmswc[:, k:k + 1], None,
                op0=ALU.mult))(),
                reads=[rWfl[fb], rRmsw], writes=[rdst], tag="sclw_a")
        fb = nwl[0] % 4
        nwl[0] += 1
        p.do("Q", (lambda k=k, fb=fb: lambda g: g.dma_start(
            out=wfl[:, fb, 0:H], in_=wg_h[k * 128:(k + 1) * 128, :]))(),
            writes=[rWfl[fb]], dma=True, tag="ldwg")
        p.do("V", (lambda k=k, fb=fb: lambda v: v.tensor_scalar(
            wvg_sb[:, k, H * DH:H * DH + H], wfl[:, fb, 0:H], rmswc[:, k:k + 1], None,
            op0=ALU.mult))(),
            reads=[rWfl[fb], rRmsw], writes=[rWvg], tag="sclwg")

    def _a0_post(tt):
        stats_tile(8 * T, tt)
        if tt == T // 2 - 1:
            stats_math(8 * T, rms=True, lo=0, hi=T // 2)
        if tt == T - 1:
            stats_math(8 * T, rms=True, lo=T // 2, hi=T)
    ffn(0, wft, rWft, lnt, rLnt, lnTt, rLnT, post_tile=_a0_post)
    p.do("Q", lambda q: q.dma_start(
        out=wftsp_h, in_=wft.ap().rearrange("pp k dd -> pp (k dd)")),
        reads=[rWft[0], rWft[1]], dma=True, tag="wft_sp")
    f1_stack.close()
    p.fence()

    qhat = alloc(stack, "qhat", [65, H, N], BF16)
    khat = alloc(stack, "khat", [65, H, N], BF16)
    Vsb = alloc(stack, "Vsb", [128, T, H, DH + 1], BF16)
    g_tok = alloc(stack, "g_tok", [128, T, H], BF16)
    gscr = alloc(stack, "gscr", [4, 2, 128], BF16)
    p.do("V", lambda v: v.memset(
        Vsb[:, :, :, DH:DH + 1], 1.0),
        writes=[rV[t] for t in range(T)], tag="vones")
    p.do("V", lambda v: v.memset(khat[64:65, :, :], -1.0),
         writes=[rKhat[h] for h in range(H)], tag="khat1")

    a12_stack = contextlib.ExitStack()
    tTc = alloc(a12_stack, "tTc", [128, KC, 2, 512], BF16, side="right")
    rTTc = [Res(), Res()]


    def emit_tprep(sc):
        cb = sc % 2
        for i4 in range(4):
            tt = sc * 4 + i4
            bb = tt % 4
            p.do("S", (lambda tt=tt, bb=bb: lambda s: s.activation(
                lnt[:, bb, :], X[:, tt, :], AF.Identity,
                bias=stat_col(8 * T, 3, tt), scale=stat_col(8 * T, 2, tt)))(),
                reads=[rX[tt], rSt(8 * T, tt // (T // 2))],
                writes=[rLnt[bb]], tag="t_ln")
        for i4 in range(4):
            tt = sc * 4 + i4
            bb = tt % 4
            hb = tt % 2
            for k in range(KC):
                p.do("T", (lambda bb=bb, hb=hb, k=k: lambda t: t.transpose(
                    psbf[:, hb * 1024 + k * 128: hb * 1024 + (k + 1) * 128],
                    lnt[:, bb, k * 128:(k + 1) * 128], identbf[:, :]))(),
                    reads=[rLnt[bb], rConst], writes=[rPB[hb]], tag="t_tr")
            p.do("V", (lambda cb=cb, hb=hb, i4=i4: lambda v: v.tensor_copy(
                tTc[:, :, cb, i4 * 128:(i4 + 1) * 128],
                psbf[:, hb * 1024:(hb + 1) * 1024]
                .rearrange("pp (k s) -> pp k s", k=KC)))(),
                reads=[rPB[hb]], writes=[rTTc[cb]], tag="t_ev")

    def emit_qkvg(sc):
        cb = sc % 2
        for h in range(H):
            bq = (h % 2) * 2
            bk = bq + 1
            for k in range(KC):
                p.do("T", (lambda h=h, k=k, bq=bq, cb=cb: lambda t: t.matmul(
                    bank(bq)[0:64, :], wq_sb[:, k, h * DH:(h + 1) * DH],
                    tTc[:, k, cb, :], start=(k == 0), stop=(k == KC - 1)))(),
                    reads=[rWq, rTTc[cb]], writes=[rPS[bq]], tag="qk_q")
                p.do("T", (lambda h=h, k=k, bk=bk, cb=cb: lambda t: t.matmul(
                    bank(bk)[0:64, :], wk_sb[:, k, h * DH:(h + 1) * DH],
                    tTc[:, k, cb, :], start=(k == 0), stop=(k == KC - 1)))(),
                    reads=[rWk, rTTc[cb]], writes=[rPS[bk]], tag="qk_k")
            p.do("V", (lambda h=h, sc=sc, bq=bq: lambda v: v.tensor_scalar_mul(
                qhat[0:64, h, sc * 512:(sc + 1) * 512], bank(bq)[0:64, :], 0.125))(),
                reads=[rPS[bq]], writes=[rQhat[h]], tag="qk_evq")
            p.do("V", (lambda h=h, sc=sc, bk=bk: lambda v: v.tensor_copy(
                khat[0:64, h, sc * 512:(sc + 1) * 512], bank(bk)[0:64, :]))(),
                reads=[rPS[bk]], writes=[rKhat[h]], tag="qk_evk")
        for i4 in range(4):
            tt = sc * 4 + i4
            bv = 4 + (i4 % 2)
            for k in range(KC):
                p.do("T", (lambda k=k, bv=bv, cb=cb, i4=i4: lambda t: t.matmul(
                    bank(bv)[:, :], tTc[:, k, cb, i4 * 128:(i4 + 1) * 128],
                    wvg_sb[:, k, 0:H * DH], start=(k == 0), stop=(k == KC - 1)))(),
                    reads=[rTTc[cb], rWvg], writes=[rPS[bv]], tag="v_mm")
                p.do("T", (lambda k=k, cb=cb, i4=i4: lambda t: t.matmul(
                    bank(0)[:, 0:H], tTc[:, k, cb, i4 * 128:(i4 + 1) * 128],
                    wvg_sb[:, k, H * DH:H * DH + H], start=(k == 0), stop=(k == KC - 1)))(),
                    reads=[rTTc[cb], rWvg], writes=[rPS[0]], tag="g_mm")
            p.do("V", (lambda tt=tt, bv=bv: lambda v: v.tensor_copy(
                Vsb[:, tt, :, 0:DH],
                bank(bv).rearrange("pp (h e) -> pp h e", h=H)))(),
                reads=[rPS[bv]], writes=[rV[tt]], tag="v_ev")
            p.do("S", (lambda tt=tt: lambda s: s.activation(
                g_tok[:, tt, :], bank(0)[:, 0:H], AF.Sigmoid))(),
                reads=[rPS[0]], writes=[rGtok], tag="g_ev")

    for sc in range(IC):
        emit_tprep(sc)
        if sc >= 1:
            emit_qkvg(sc - 1)
    emit_qkvg(IC - 1)
    a12_stack.close()
    aw_stack.close()
    p.fence(exclude=("V",))

    a3_stack = contextlib.ExitStack()
    wo_bf = nc.dram_tensor("wo_bf", [H, DH, D], BF16, kind="Internal").ap()
    wobuf = alloc(a3_stack, "wobuf", [64, 4, D], BF16, side="right")
    outev = alloc(a3_stack, "outev", [64, 2, 512], F32, side="right")
    outs = alloc(a3_stack, "outs", [64, H, 512], BF16, side="right")
    Lrow = alloc(a3_stack, "Lrow", [65, 2, 512], F32, side="right")
    gRow = alloc(a3_stack, "gRow", [65, 2, 512], BF16, side="right")
    c2row = alloc(a3_stack, "c2row", [65, 2, 512], F32, side="right")
    c2bf = alloc(a3_stack, "c2bf", [65, 2, 512], BF16, side="right")
    c2full = alloc(a3_stack, "c2full", [64, 2, 512], BF16, side="right")
    ones64 = alloc(a3_stack, "ones64", [65, 64], BF16, side="right")
    lnmk = alloc(a3_stack, "lnmk", [1, H], F32, side="right")
    bnd_stack = contextlib.ExitStack()
    qks = alloc(bnd_stack, "qks", [64, 2, N], BF16, side="right")
    rowA = alloc(bnd_stack, "rowA", [1, 512], F32, side="right")
    pmk = alloc(bnd_stack, "pmk", [1, 8], F32, side="right")
    rWo = Res()
    rWobuf = [Res() for _ in range(4)]
    rPt = [Res() for _ in range(4)]
    rC2 = Res()
    rC2full = [Res(), Res()]
    p.do("V", lambda v: v.memset(ones64[:, :], 1.0), writes=[rConst], tag="ones64")
    rOutev = [Res(), Res()]
    rOuts = Res()
    rLrow = [Res(), Res()]
    rGrow = [Res(), Res()]
    rQks = Res()
    rRowA = Res()
    rPmk = Res()
    rLnmk = Res()

    p.do("G", lambda g: g.dma_start(
        out=wo_bf.rearrange("h e d -> (h e) d"),
        in_=wo_h.rearrange("h e d -> (h e) d")),
        writes=[rWo], dma=True, tag="wo_cast")

    ringc = [0]

    def ring1():
        b = ringc[0] % 4
        ringc[0] += 1
        return b

    def ring2():
        if ringc[0] % 2:
            ringc[0] += 1
        b = ringc[0] % 4
        ringc[0] += 2
        return b

    def emit_bound(h):
        ext = ([rTTc[0], rTTc[1], rWq, rWk, rWvg] + rWfl) if h == 0 else []
        p.do("V", (lambda h=h: lambda v: v.tensor_mul(
            qks[:, 0, :], qhat[0:64, h, :], qhat[0:64, h, :]))(),
            reads=[rQhat[h]] + ext, writes=[rQks], tag="b_qsq")
        p.do("V", (lambda h=h: lambda v: v.tensor_mul(
            qks[:, 1, :], khat[0:64, h, :], khat[0:64, h, :]))(),
            reads=[rKhat[h]], writes=[rQks], tag="b_ksq")
        kslots = []
        for icc in range(IC):
            b = ring1()
            kslots.append(b)
            p.do("T", (lambda icc=icc, b=b: lambda t: t.matmul(
                bank(b)[0:1, :], onesc[0:64, 0:1], qks[:, 1, icc * 512:(icc + 1) * 512],
                start=True, stop=True))(),
                reads=[rQks, rConst], writes=[rPS[b]], tag="b_kmm")
        for icc, b in enumerate(kslots):
            p.do("V", (lambda icc=icc, b=b: lambda v: v.reduce_max(
                pmk[0:1, icc:icc + 1], bank(b)[0:1, :], axis=AX.X))(),
                reads=[rPS[b]], writes=[rPmk], tag="b_kpm")
        p.do("V", (lambda: lambda v: v.reduce_max(
            pmk[0:1, IC:IC + 1], pmk[0:1, 0:IC], axis=AX.X))(),
            reads=[rPmk], writes=[rPmk], tag="b_kmx")
        p.do("S", (lambda h=h: lambda s: s.activation(
            pmk[0:1, IC + 1:IC + 2], pmk[0:1, IC:IC + 1], AF.Ln, bias=1e-30))(),
            reads=[rPmk], writes=[rPmk], tag="b_lnk")
        p.do("V", (lambda h=h: lambda v: v.tensor_scalar_mul(
            lnmk[0:1, h:h + 1], pmk[0:1, IC + 1:IC + 2], 0.5))(),
            reads=[rPmk], writes=[rLnmk], tag="b_lnk2")
        for icc in range(IC):
            b = ring1()
            p.do("T", (lambda icc=icc, b=b: lambda t: t.matmul(
                bank(b)[0:1, :], onesc[0:64, 0:1], qks[:, 0, icc * 512:(icc + 1) * 512],
                start=True, stop=True))(),
                reads=[rQks, rConst], writes=[rPS[b]], tag="b_qmm")
            p.do("S", (lambda icc=icc, b=b: lambda s: s.activation(
                rowA[0:1, :], bank(b)[0:1, :], AF.Ln, bias=1e-30))(),
                reads=[rPS[b]], writes=[rRowA], tag="b_lnq")
            p.do("S", (lambda h=h, icc=icc: lambda s: s.activation(
                qhat[64:65, h, icc * 512:(icc + 1) * 512], rowA[0:1, :], AF.Exp,
                scale=0.5, bias=lnmk[0:1, h:h + 1]))(),
                reads=[rRowA, rLnmk], writes=[rQhat[h]], tag="b_c")

    pending_gate = [None]
    pending_gate_t = [None]
    pend_pv = [None]
    pend_epi = [None]

    def flush_gate_t():
        if pending_gate_t[0] is not None:
            pending_gate_t[0]()
            pending_gate_t[0] = None

    def flush_gate():
        flush_gate_t()
        if pending_gate[0] is not None:
            pending_gate[0]()
            pending_gate[0] = None

    def flush_tail():
        if pend_pv[0] is not None:
            pend_pv[0]()
            pend_pv[0] = None
        if pend_epi[0] is not None:
            pend_epi[0]()
            pend_epi[0] = None

    def emit_pass2(ic, h):
        ob = h % 2
        ab = 4 + (h % 2)

        def emit_grow(h=h, ic=ic, ob=ob):
            gp = h % 2
            gof = gp * 1024
            p.do("T", (lambda h=h, ic=ic, gof=gof: lambda t: t.transpose(
                psbf[0:4, gof:gof + 128], g_tok[:, 4 * ic:4 * ic + 4, h],
                identbf[:, :]))(),
                reads=[rGtok, rConst], writes=[rPB[gp]], tag="g_tr")
            p.do("V", (lambda gp=gp, gof=gof: lambda v: v.tensor_copy(
                gscr[:, gp, :], psbf[0:4, gof:gof + 128]))(),
                reads=[rPB[gp]], writes=[rGscr], tag="g_scr")
            p.do("Q", (lambda ob=ob, gp=gp: lambda q: q.dma_start(
                out=gRow[64:65, ob, :].rearrange("o (t f) -> o t f", t=4),
                in_=gscr[:, gp, :]))(),
                reads=[rGscr], writes=[rGrow[ob]], dma=True, chain=0, tag="g_dma")
        npair = 2 * (ic + 1)

        def emit_pv(pg, pb):
            for b in range(2):
                jt = 2 * pg + b
                first = (pg == 0 and b == 0)
                last = (pg == npair - 1 and b == 1)
                coff = 0 if (first or last) else max(jt - 4 * ic, 0) * 128
                p.do("T", (lambda h=h, jt=jt, b=b, pb=pb, ab=ab, coff=coff,
                           first=first, last=last: lambda t: t.matmul(
                    bank(ab)[0:65, coff:512], Vsb[:, jt, h, :],
                    pt[:, pb, b * 512 + coff:(b + 1) * 512],
                    start=first, stop=last, skip_group_check=(coff > 0)))(),
                    reads=[rPt[pb], rV[jt]], writes=[rPS[ab]], tag="pv_mm")

        def epilogue(h=h, ob=ob, ab=ab):
            p.do("V", (lambda ob=ob, ab=ab: lambda v: v.tensor_copy(
                outev[:, ob, :], bank(ab)[0:64, :]))(),
                reads=[rPS[ab]], writes=[rOutev[ob]], tag="pv_ev")
            if h == H - 1:
                flush_gate()
                p.do("S", (lambda ob=ob, ab=ab: lambda s: s.activation(
                    c2row[64:65, ob, :], bank(ab)[64:65, :], AF.Ln))(),
                    reads=[rPS[ab]], writes=[rC2], tag="c2ln_f")
                p.do("S", (lambda ob=ob: lambda s: s.activation(
                    c2row[64:65, ob, :], c2row[64:65, ob, :], AF.Exp, scale=-1.0))(),
                    reads=[rC2], writes=[rC2], tag="c2exp_f")
                p.do("V", (lambda ob=ob: lambda v: v.tensor_mul(
                    c2bf[64:65, ob, :], c2row[64:65, ob, :], gRow[64:65, ob, :]))(),
                    reads=[rGrow[ob]], writes=[rC2], tag="c2_f")
                cb_b = ring1()
                p.do("T", (lambda ob=ob, cb_b=cb_b: lambda t: t.matmul(
                    bank(cb_b)[0:64, :], ones64[64:65, 0:64], c2bf[64:65, ob, :],
                    start=True, stop=True))(),
                    reads=[rC2, rConst], writes=[rPS[cb_b]], tag="cb_mm_f")
                p.do("V", (lambda h=h, ob=ob, cb_b=cb_b: lambda v: v.tensor_tensor(
                    outs[:, h, :], outev[:, ob, :], bank(cb_b)[0:64, :],
                    op=ALU.mult))(),
                    reads=[rOutev[ob], rPS[cb_b]], writes=[rOuts], tag="c_scale_f")
                pending_gate[0] = None
                return
            p.do("V", (lambda ob=ob, ab=ab: lambda v: v.tensor_copy(
                Lrow[64:65, ob, :], bank(ab)[64:65, :]))(),
                reads=[rPS[ab]], writes=[rLrow[ob]], tag="l_ev")

            def gate(h=h, ob=ob):
                p.do("S", (lambda ob=ob: lambda s: s.activation(
                    c2row[64:65, ob, :], Lrow[64:65, ob, :], AF.Ln))(),
                    reads=[rLrow[ob]], writes=[rC2], tag="c2ln")
                p.do("S", (lambda ob=ob: lambda s: s.activation(
                    c2row[64:65, ob, :], c2row[64:65, ob, :], AF.Exp, scale=-1.0))(),
                    reads=[rC2], writes=[rC2], tag="c2exp")
                p.do("V", (lambda ob=ob: lambda v: v.tensor_mul(
                    c2bf[64:65, ob, :], c2row[64:65, ob, :], gRow[64:65, ob, :]))(),
                    reads=[rGrow[ob]], writes=[rC2], tag="c2")

                def _bc(ob=ob):
                    def run(q):
                        s = c2bf[64:65, ob, :]
                        bsrc = AP(s.tensor, s.offset,
                                  [(s.ap[0][0], 1), (0, 64)] + list(s.ap[1:]))
                        return q.dma_start(out=c2full[:, ob, :], in_=bsrc)
                    return run
                p.do("Q", _bc(), reads=[rC2], writes=[rC2full[ob]],
                     dma=True, chain=1, tag="c2_bc")

                def gate_t(h=h, ob=ob):
                    p.do("V", (lambda h=h, ob=ob: lambda v: v.tensor_tensor(
                        outs[:, h, :], outev[:, ob, :], c2full[:, ob, :],
                        op=ALU.mult))(),
                        reads=[rOutev[ob], rC2full[ob]], writes=[rOuts], tag="c_scale")
                pending_gate_t[0] = gate_t
            flush_gate()
            pending_gate[0] = gate

        lpv = []
        for pg in range(npair):
            pb = pg % 4
            if pg == 1:
                flush_tail()
                emit_grow()
            b0 = ring2()
            for b in range(2):
                jt = 2 * pg + b
                p.do("T", (lambda h=h, ic=ic, jt=jt, b0=b0, b=b:
                           lambda t: t.matmul(
                    bank(b0 + b)[:, :], khat[:, h, jt * 128:(jt + 1) * 128],
                    qhat[:, h, ic * 512:(ic + 1) * 512],
                    start=True, stop=True))(),
                    reads=[rQhat[h], rKhat[h]], writes=[rPS[b0 + b]], tag="s2_mm")
            p.do("S", (lambda pb=pb, b0=b0: lambda s: s.activation(
                pt[:, pb, :], banks(b0, 2), AF.Exp))(),
                reads=[rPS[b0], rPS[b0 + 1]], writes=[rPt[pb]], tag="s2_exp")
            for b in range(2):
                jt = 2 * pg + b
                r = jt - 4 * ic
                if r >= 0:
                    if r > 0 and pg == npair - 1 and b == 1:
                        p.do("V", (lambda pb=pb, b=b, r=r: lambda v: v.memset(
                            pt[:, pb, b * 512: b * 512 + r * 128], 0.0))(),
                            writes=[rPt[pb]], tag="z_slab")
                    p.do("V", (lambda pb=pb, b=b, r=r: lambda v: v.tensor_mul(
                        pt[:, pb, b * 512 + r * 128: b * 512 + (r + 1) * 128],
                        pt[:, pb, b * 512 + r * 128: b * 512 + (r + 1) * 128],
                        ubin[:, :]))(),
                        reads=[rConst], writes=[rPt[pb]], tag="z_tri")
            if pg == 0:
                flush_gate_t()
            lpv.append((pg, pb))
            if len(lpv) > 2:
                emit_pv(*lpv.pop(0))
        if len(lpv) > 1:
            emit_pv(*lpv.pop(0))
        pend_pv[0] = (lambda a=lpv.pop(0): lambda: emit_pv(*a))()
        pend_epi[0] = epilogue

    def emit_wo(ic):
        flush_gate_t()
        flush_tail()
        flush_gate()
        flush_gate_t()
        for ph in range(2):
            for h in range(H):
                wb = h % 4
                p.do("Q", (lambda h=h, wb=wb: lambda q: q.dma_start(
                    out=wobuf[:, wb, :], in_=wo_bf[h, :, :]))(),
                    reads=[rWo], writes=[rWobuf[wb]], dma=True, chain=2 + (h % 2),
                    tag="ldwo")
                for i2 in range(2):
                    pb0 = i2 * 2
                    itr = ph * 2 + i2
                    for hh in range(2):
                        p.do("T", (lambda h=h, wb=wb, itr=itr, hh=hh, pb0=pb0:
                                   lambda t: t.matmul(
                            bank(pb0 + hh)[:, :], outs[:, h, itr * 128:(itr + 1) * 128],
                            wobuf[:, wb, hh * 512:(hh + 1) * 512],
                            start=(h == 0), stop=(h == H - 1)))(),
                            reads=[rOuts, rWobuf[wb]], writes=[rPS[pb0 + hh]], tag="wo_mm")
            for i2 in range(2):
                itr = ph * 2 + i2
                tt = 4 * ic + itr
                pb0 = i2 * 2
                for hh in range(2):
                    p.do("V", (lambda tt=tt, hh=hh, pb0=pb0: lambda v: v.tensor_add(
                        X[:, tt, hh * 512:(hh + 1) * 512],
                        X[:, tt, hh * 512:(hh + 1) * 512], bank(pb0 + hh)[:, :]))(),
                        reads=[rPS[pb0 + hh], rX[tt]], writes=[rX[tt]], tag="wo_res")
                stats_tile(16 * T, tt)

    for h in range(H):
        emit_bound(h)
    bnd_stack.close()
    pt = alloc(a3_stack, "pt", [128, 4, 1024], BF16, side="right")
    for ic in range(IC):
        for h in range(H):
            emit_pass2(ic, h)
        emit_wo(ic)
    a3_stack.close()
    p.fence()

    stats_math(16 * T, lo=0, hi=T // 2)
    stats_math(16 * T, lo=T // 2, hi=T)

    f2_stack = contextlib.ExitStack()
    wft2 = alloc(f2_stack, "wft2", [128, KC, D], BF16, side="right")
    lnTt2 = alloc(f2_stack, "lnT2", [128, 2, KC, 128], BF16, side="right")
    rWft2, rLnT2 = [Res(), Res()], [Res(), Res()]
    p.do("Q", lambda q: q.dma_start(
        out=wft2.ap().rearrange("pp k dd -> pp (k dd)"), in_=wftsp_h),
        writes=[rWft2[0], rWft2[1]], dma=True, tag="wft_rl")
    ffn(16 * T, wft2, rWft2, lnt, rLnt, lnTt2, rLnT2)

    for tt in range(T):
        p.do("Q", (lambda tt=tt: lambda q: q.dma_start(
            out=out_h.rearrange("(t pp) d -> pp t d", pp=128)[:, tt, :],
            in_=X[:, tt, :]))(),
            reads=[rX[tt]], dma=True, tag="stX")
    f2_stack.close()

    totals = p.finalize()

    sem_keys = ["T", "V", "S"] + [f"{e}{c}" for e in ("G", "Q") for c in range(Prog.NCHAIN)]
    sems = {k: stack.enter_context(nc.semaphore("sem" + k)) for k in sem_keys}

    with nc.Block() as block:
        def make_body(ename):
            def body(eng):
                for op in p.ops[ename]:
                    if ename not in ("G", "Q") and op.drain_before:
                        eng.drain()
                    for pe, v in op.waits:
                        eng.wait_ge(sems[pe], v)
                    inst = op.emit(eng)
                    if op.need_inc:
                        inst.then_inc(sems[op.sem_key], 16 if op.dma else 1)
                if ename == "Q":
                    for c in range(Prog.NCHAIN):
                        if totals.get(f"Q{c}", 0):
                            eng.wait_ge(sems[f"Q{c}"], totals[f"Q{c}"])
                        if totals.get(f"G{c}", 0):
                            eng.wait_ge(sems[f"G{c}"], totals[f"G{c}"])
            return body
        block.tensor(make_body("T"))
        block.vector(make_body("V"))
        block.scalar(make_body("S"))
        block.gpsimd(make_body("G"))
        block.sync(make_body("Q"))

    stack.close()
    return nc


def _consts():
    ident = np.eye(128)
    f = np.arange(128)[None, :]
    pp = np.arange(128)[:, None]
    consts = {
        "identbf": ident.astype(BF),
        "identf": ident.astype(np.float32),
        "umask": np.where(f > pp, NEGBIG, 0.0).astype(BF),
        "ubin": (f >= pp).astype(np.float32).astype(BF),
    }
    return consts


def kernel(L0_attn_tokens, L0_attn_wq, L0_attn_wk, L0_attn_wv, L0_attn_wo,
           L0_attn_w_lr, L0_attn_w_target, L0_attn_w_gates, L0_attn_rms_w,
           L0_ffn_weight):
    B, N, D_ = L0_attn_tokens.shape
    assert D_ == D
    nc = build_program(N)
    common = {
        "wq": np.ascontiguousarray(L0_attn_wq, np.float32),
        "wk": np.ascontiguousarray(L0_attn_wk, np.float32),
        "wv": np.ascontiguousarray(L0_attn_wv, np.float32),
        "wo": np.ascontiguousarray(L0_attn_wo, np.float32),
        "wg": np.ascontiguousarray(L0_attn_w_gates, np.float32),
        "rmsw": np.ascontiguousarray(np.asarray(L0_attn_rms_w, np.float32).reshape(KC, 128).T),
        "wf": np.ascontiguousarray(L0_ffn_weight, np.float32),
    }
    common.update(_consts())
    in_maps = []
    for c in range(B):
        m = dict(common)
        m["x"] = np.ascontiguousarray(L0_attn_tokens[c], np.float32)
        in_maps.append(m)
    res = run_bass_kernel_spmd(nc, in_maps, core_ids=list(range(B)))
    return np.stack([res.results[c]["out"] for c in range(B)], axis=0)


# version 45
# speedup vs baseline: 1.0170x; 1.0170x over previous
import contextlib
import numpy as np
import ml_dtypes

import concourse.bass as bass
import concourse.mybir as mybir
from concourse.bass_utils import run_bass_kernel_spmd
from concourse.bass_types import AP

F32 = mybir.dt.float32
BF16 = mybir.dt.bfloat16
AF = mybir.ActivationFunctionType
ALU = mybir.AluOpType
AX = mybir.AxisListType

D = 1024
H = 8
DH = 64
KC = 8
NEGBIG = -3.0e38
EPS_LN = 1e-5
EPS_RMS = float(np.finfo(np.float32).eps)
BF = ml_dtypes.bfloat16


class Op:
    __slots__ = ("eng", "emit", "deps", "dma", "need_inc", "inc_val", "waits", "tag", "sem_key", "idx", "drain_before", "chain")

    def __init__(self, eng, emit, deps, dma, tag):
        self.eng = eng
        self.emit = emit
        self.deps = deps
        self.dma = dma
        self.need_inc = dma
        self.inc_val = 0
        self.waits = None
        self.tag = tag
        self.drain_before = False
        self.chain = None


class Res:
    __slots__ = ("writers", "readers")

    def __init__(self):
        self.writers = []
        self.readers = []

    def rdeps(self):
        return [(w, True) for w in self.writers]

    def wdeps(self):
        return [(r, True) for r in self.readers] + [(w, False) for w in self.writers]

    def note_read(self, op):
        self.readers.append(op)

    def note_write(self, op):
        if self.readers:
            self.writers = [op]
            self.readers = []
        else:
            self.writers.append(op)


class Prog:
    ENGS = ("T", "V", "S", "G", "Q")

    def __init__(self, race_safe=False):
        self.ops = {e: [] for e in self.ENGS}
        self.fence_deps = []
        self.fence_exclude = ()
        self.auto_deps = {}
        self.race_safe = race_safe

    def do(self, eng, emit, reads=(), writes=(), dma=False, tag="", chain=None):
        deps = [] if eng in self.fence_exclude else \
            [(d, True) for d in self.fence_deps]
        for r in self.auto_deps.get(eng, ()):
            deps += r.rdeps()
        for r in reads:
            deps += r.rdeps()
        for w in writes:
            deps += w.wdeps()
        op = Op(eng, emit, deps, dma, tag)
        op.chain = chain
        op.idx = len(self.ops[eng])
        self.ops[eng].append(op)
        for r in reads:
            r.note_read(op)
        for w in writes:
            w.note_write(op)
        return op

    def fence(self, exclude=()):
        self.fence_deps = [self.ops[e][-1] for e in self.ENGS if self.ops[e]]
        self.fence_exclude = exclude


    NCHAIN = 4

    def finalize(self):
        for e in self.ENGS:
            for op in self.ops[e]:
                for d, _hz in op.deps:
                    if d.eng != op.eng or d.dma:
                        d.need_inc = True
        totals = {}
        for e in self.ENGS:
            if e in ("G", "Q"):
                nd = 0
                chain_vals = [0] * self.NCHAIN
                for op in self.ops[e]:
                    if op.chain is not None:
                        c = op.chain % self.NCHAIN
                    else:
                        c = nd % self.NCHAIN
                        nd += 1
                    chain_vals[c] += 16
                    op.sem_key = f"{e}{c}"
                    op.inc_val = chain_vals[c]
                for c in range(self.NCHAIN):
                    totals[f"{e}{c}"] = chain_vals[c]
            else:
                cnt = 0
                for op in self.ops[e]:
                    if op.need_inc:
                        cnt += 1
                    op.sem_key = e
                    op.inc_val = cnt
                totals[e] = cnt
        for e in self.ENGS:
            hwm = {}
            for op in self.ops[e]:
                w = {}
                if op.dma:
                    if op.inc_val > 16:
                        w[op.sem_key] = op.inc_val - 16
                for dd, _hz in op.deps:
                    if dd.eng == op.eng and not dd.dma:
                        continue
                    if dd.inc_val > w.get(dd.sem_key, 0):
                        w[dd.sem_key] = dd.inc_val
                op.waits = [(pe, v) for pe, v in w.items() if v > hwm.get(pe, 0)]
                for pe, v in op.waits:
                    hwm[pe] = v
        drain_engines = ("V", "S") if self.race_safe else ("V",)
        for e in self.ENGS:
            if e not in drain_engines:
                continue
            drained_to = -1
            for op in self.ops[e]:
                need = -1
                for dd, hz in op.deps:
                    if (hz or self.race_safe) and dd.eng == e and not dd.dma \
                            and dd.idx > need:
                        need = dd.idx
                if need > drained_to:
                    op.drain_before = True
                    drained_to = op.idx - 1
                else:
                    op.drain_before = False
        return totals


def build_program(N, race_safe=False):
    T = N // 128
    IC = N // 512
    assert IC * 512 == N and T % 4 == 0

    nc = bass.Bass("TRN2", target_bir_lowering=False, debug=False,
                   detect_race_conditions=race_safe)

    x_h = nc.dram_tensor("x", [N, D], F32, kind="ExternalInput").ap()
    wq_h = nc.dram_tensor("wq", [H, D, DH], F32, kind="ExternalInput").ap()
    wk_h = nc.dram_tensor("wk", [H, D, DH], F32, kind="ExternalInput").ap()
    wv_h = nc.dram_tensor("wv", [H, D, DH], F32, kind="ExternalInput").ap()
    wo_h = nc.dram_tensor("wo", [H, DH, D], F32, kind="ExternalInput").ap()
    wg_h = nc.dram_tensor("wg", [D, H], F32, kind="ExternalInput").ap()
    rmsw_h = nc.dram_tensor("rmsw", [128, KC], F32, kind="ExternalInput").ap()
    wf_h = nc.dram_tensor("wf", [D, D], F32, kind="ExternalInput").ap()
    identbf_h = nc.dram_tensor("identbf", [128, 128], BF16, kind="ExternalInput").ap()
    identf_h = nc.dram_tensor("identf", [128, 128], F32, kind="ExternalInput").ap()
    umask_h = nc.dram_tensor("umask", [128, 128], BF16, kind="ExternalInput").ap()
    ubin_h = nc.dram_tensor("ubin", [128, 128], BF16, kind="ExternalInput").ap()
    out_h = nc.dram_tensor("out", [N, D], F32, kind="ExternalOutput").ap()

    p = Prog(race_safe)
    stack = contextlib.ExitStack()

    def alloc(st, name, shape, dtype, side="left"):
        return st.enter_context(nc.sbuf_tensor(name, shape, dtype, side=side))

    X = alloc(stack, "X", [128, T, D], F32)
    identbf = alloc(stack, "identbf_s", [128, 128], BF16)
    identf = alloc(stack, "identf_s", [128, 128], F32)
    umask = alloc(stack, "umask_s", [128, 128], BF16)
    ubin = alloc(stack, "ubin_s", [128, 128], BF16)
    stat = alloc(stack, "stat", [128, 24 * T], F32)
    rmswc = alloc(stack, "rmswc", [128, KC], F32)
    sq_scr = alloc(stack, "sq_scr", [128, D], BF16)
    ones1 = alloc(stack, "ones1", [1, 128], BF16)
    onesc = alloc(stack, "onesc", [65, 1], BF16)
    CONST_VALS = [EPS_LN, EPS_RMS, 0.125, 0.5, -1.0, 1.0 / D, 1.0, 0.0, 1e-30]
    cdb = alloc(stack, "cdb", [128, len(CONST_VALS)], F32)
    for ci, cv in enumerate(CONST_VALS):
        nc.const_aps.register(F32, cv, cdb[:, ci:ci + 1])
    epsr = alloc(stack, "epsr", [128, 1], F32)

    ps8 = stack.enter_context(nc.psum_tensor("ps8", [128, 3072], F32))
    psbf = stack.enter_context(nc.psum_tensor("psbf", [128, 2048], BF16))

    rX = [Res() for _ in range(T)]
    rPS = [Res() for _ in range(6)]
    rPB = [Res() for _ in range(2)]
    rStatK = {}

    def rSt(base, half):
        return rStatK.setdefault((base, half), Res())
    rStSum = Res()
    rStSq = Res()
    rQhat = [Res() for _ in range(H)]
    rKhat = [Res() for _ in range(H)]
    rV = [Res() for _ in range(T)]
    rGtok = Res()
    rLscr = Res()
    rGscr = Res()
    rRmsw = Res()
    rConst = Res()
    rSq = Res()

    def bank(b):
        return ps8[:, b * 512:(b + 1) * 512]

    def banks(b0, nb):
        return ps8[:, b0 * 512:(b0 + nb) * 512]

    for dst, src in ((identbf, identbf_h), (identf, identf_h), (umask, umask_h),
                     (ubin, ubin_h)):
        p.do("Q", (lambda dst=dst, src=src: lambda q: q.dma_start(
            out=dst[:, :], in_=src[:, :]))(), writes=[rConst], dma=True, tag="ldc")
    xstep = max(2, T // 8)
    for tt in range(0, T, xstep):
        p.do("Q", (lambda tt=tt, xstep=xstep: lambda q: q.dma_start(
            out=X[:, tt:tt + xstep, :],
            in_=x_h.rearrange("(t pp) d -> pp t d", pp=128)[:, tt:tt + xstep, :]))(),
            writes=[rX[tt + i] for i in range(xstep)], dma=True, tag="ldX")
    p.do("Q", lambda q: q.dma_start(
        out=rmswc[:, :], in_=rmsw_h[:, :]),
        writes=[rRmsw], dma=True, tag="ldrmsw")
    rCAp = Res()
    for ci, cv in enumerate(CONST_VALS):
        p.do("V", (lambda ci=ci, cv=cv: lambda v: v.memset(cdb[:, ci:ci + 1], cv))(),
             writes=[rCAp], tag="cdb")
    p.auto_deps = {"S": [rCAp], "V": [rCAp], "T": [rCAp]}
    p.do("V", lambda v: v.memset(ones1[:, :], 1.0), writes=[rConst], tag="ones1")
    p.do("V", lambda v: v.memset(onesc[:, :], 1.0), writes=[rConst], tag="onesc")

    rEps = Res()
    one_c = cdb[:, 6:7]
    p.do("V", lambda v: v.memset(epsr[:, :], 3.0), writes=[rEps], tag="eps0")
    p.do("V", lambda v: v.reciprocal(epsr[:, :], epsr[:, :]),
         reads=[rEps], writes=[rEps], tag="eps1")
    p.do("V", lambda v: v.tensor_scalar_mul(epsr[:, :], epsr[:, :], 4.0),
         reads=[rEps], writes=[rEps], tag="eps2")
    p.do("V", lambda v: v.scalar_tensor_tensor(
        epsr[:, :], one_c, -1.0, epsr[:, :], op0=ALU.mult, op1=ALU.add),
        reads=[rEps], writes=[rEps], tag="eps3")
    p.do("V", lambda v: v.tensor_scalar_mul(epsr[:, :], epsr[:, :], 3.0),
         reads=[rEps], writes=[rEps], tag="eps4")
    p.do("V", lambda v: v.scalar_tensor_tensor(
        epsr[:, :], one_c, -1.0, epsr[:, :], op0=ALU.mult, op1=ALU.add),
        reads=[rEps], writes=[rEps], tag="eps5")
    p.do("V", lambda v: v.tensor_mul(epsr[:, :], epsr[:, :], epsr[:, :]),
        reads=[rEps], writes=[rEps], tag="eps6")
    p.do("S", lambda s: s.activation(epsr[:, :], epsr[:, :], AF.Sqrt),
         reads=[rEps], writes=[rEps], tag="eps7")

    def build_wft(wft, rWft, wld, rWld, wsg, rWsg, tagp):
        for oc in range(KC):
            wb = oc % 2
            p.do("G", (lambda oc=oc, wb=wb: lambda g: g.dma_start(
                out=wld[:, wb, :], in_=wf_h[oc * 128:(oc + 1) * 128, :]))(),
                writes=[rWld[wb]], dma=True, tag="ldW" + tagp)
            p.do("S", (lambda wb=wb: lambda s: s.sign(wsg[:, :], wld[:, wb, :]))(),
                 reads=[rWld[wb]], writes=[rWsg], tag="sgnW" + tagp)
            pb = oc % 2
            for k in range(KC):
                p.do("T", (lambda oc=oc, k=k, pb=pb: lambda t: t.transpose(
                    psbf[:, pb * 1024 + k * 128: pb * 1024 + (k + 1) * 128],
                    wsg[:, k * 128:(k + 1) * 128], identbf[:, :]))(),
                    reads=[rWsg, rConst], writes=[rPB[pb]], tag="trW" + tagp)
            p.do("V", (lambda oc=oc, pb=pb: lambda v: v.tensor_copy(
                wft[:, :, oc * 128:(oc + 1) * 128], psbf[:, pb * 1024:(pb + 1) * 1024]
                .rearrange("pp (k s) -> pp k s", k=KC)))(),
                reads=[rPB[pb]], writes=[rWft[oc // 4]], tag="evW" + tagp)

    def stats_tile(base, tt):
        p.do("V", (lambda base=base, tt=tt: lambda v: v.reduce_sum(
            stat[:, base + 4 * T + tt: base + 4 * T + tt + 1], X[:, tt, :],
            axis=AX.X))(),
            reads=[rX[tt]], writes=[rStSum], tag="lnsum")
        p.do("S", (lambda base=base, tt=tt: lambda s: s.activation(
            sq_scr[:, :], X[:, tt, :], AF.Square,
            accum_out=stat[:, base + 5 * T + tt: base + 5 * T + tt + 1]))(),
            reads=[rX[tt]], writes=[rStSq, rSq], tag="lnsq")

    def stats_math(base, rms=False, lo=0, hi=None):
        if hi is None:
            hi = T
        rStat = rSt(base, 0 if lo == 0 else 1)
        ST = lambda j: stat[:, base + j * T + lo: base + j * T + hi]
        p.do("V", lambda v: v.tensor_scalar_mul(ST(0), ST(4), 1.0 / D),
             reads=[rStSum], writes=[rStat], tag="ln_m")
        p.do("V", lambda v: v.tensor_scalar_mul(ST(5), ST(5), 1.0 / D),
             reads=[rStSq], writes=[rStat], tag="ln_e2")
        p.do("V", lambda v: v.scalar_tensor_tensor(
            ST(6), ST(0), -1.0, ST(0), op0=ALU.mult, op1=ALU.mult),
            reads=[rStat], writes=[rStat], tag="ln_mm")
        p.do("V", lambda v: v.tensor_add(ST(6), ST(6), ST(5)),
             reads=[rStat], writes=[rStat], tag="ln_var")
        p.do("S", lambda s: s.activation(ST(7), ST(6), AF.Sqrt, bias=EPS_LN),
             reads=[rStat], writes=[rStat], tag="ln_sd")
        p.do("V", lambda v: v.reciprocal(ST(1), ST(7)),
             reads=[rStat], writes=[rStat], tag="ln_r")
        if rms:
            p.do("S", lambda s: s.activation(ST(7), ST(6), AF.Copy, bias=EPS_LN),
                 reads=[rStat], writes=[rStat], tag="rms_vpe")
            p.do("V", lambda v: v.reciprocal(ST(7), ST(7)),
                 reads=[rStat], writes=[rStat], tag="rms_rec")
            p.do("V", lambda v: v.tensor_mul(ST(7), ST(6), ST(7)),
                 reads=[rStat], writes=[rStat], tag="rms_E")
            p.do("S", lambda s: s.activation(ST(7), ST(7), AF.Sqrt, bias=epsr[:, 0:1]),
                 reads=[rStat, rEps], writes=[rStat], tag="rms_sq")
            p.do("V", lambda v: v.reciprocal(ST(7), ST(7)),
                 reads=[rStat], writes=[rStat], tag="rms_rec2")
            p.do("V", lambda v: v.tensor_mul(ST(2), ST(1), ST(7)),
                 reads=[rStat], writes=[rStat], tag="rms_rp")
            p.do("V", lambda v: v.scalar_tensor_tensor(
                ST(3), ST(0), -1.0, ST(2), op0=ALU.mult, op1=ALU.mult),
                reads=[rStat], writes=[rStat], tag="ln_nmr")
        else:
            p.do("V", lambda v: v.scalar_tensor_tensor(
                ST(3), ST(0), -1.0, ST(1), op0=ALU.mult, op1=ALU.mult),
                reads=[rStat], writes=[rStat], tag="ln_nmr")

    def ln_stats(base, rms=False):
        for tt in range(T):
            stats_tile(base, tt)
            if tt == T // 2 - 1:
                stats_math(base, rms, 0, T // 2)
        stats_math(base, rms, T // 2, T)

    def stat_col(base, j, tt):
        return stat[:, base + j * T + tt: base + j * T + tt + 1]

    def ffn(base, wft, rWft, lnt, rLnt, lnTt, rLnT, post_tile=None):
        def prep(tt):
            bb = tt % 4
            hb = tt % 2
            p.do("S", (lambda tt=tt, bb=bb: lambda s: s.activation(
                lnt[:, bb, :], X[:, tt, :], AF.Identity,
                bias=stat_col(base, 3, tt), scale=stat_col(base, 1, tt)))(),
                reads=[rX[tt], rSt(base, tt // (T // 2))],
                writes=[rLnt[bb]], tag="f_ln")
            for k in range(KC):
                p.do("T", (lambda bb=bb, hb=hb, k=k: lambda t: t.transpose(
                    psbf[:, hb * 1024 + k * 128: hb * 1024 + (k + 1) * 128],
                    lnt[:, bb, k * 128:(k + 1) * 128], identbf[:, :]))(),
                    reads=[rLnt[bb], rConst], writes=[rPB[hb]], tag="f_tr")
            p.do("V", (lambda hb=hb, tt=tt: lambda v: v.tensor_copy(
                lnTt[:, tt % 2, :, :], psbf[:, hb * 1024:(hb + 1) * 1024]
                .rearrange("pp (k s) -> pp k s", k=KC)))(),
                reads=[rPB[hb]], writes=[rLnT[tt % 2]], tag="f_ev")

        def mms(tt):
            bb = tt % 2
            yb = (tt % 2) * 2
            for k in range(KC):
                for hh in range(2):
                    p.do("T", (lambda bb=bb, k=k, hh=hh, yb=yb: lambda t: t.matmul(
                        bank(yb + hh)[:, :], lnTt[:, bb, k, :],
                        wft[:, k, hh * 512:(hh + 1) * 512],
                        start=(k == 0), stop=(k == KC - 1)))(),
                        reads=[rLnT[bb], rWft[hh]], writes=[rPS[yb + hh]], tag="f_mm")
            for hh in range(2):
                p.do("V", (lambda tt=tt, hh=hh, yb=yb: lambda v: v.scalar_tensor_tensor(
                    X[:, tt, hh * 512:(hh + 1) * 512], bank(yb + hh)[:, :], 0.5,
                    X[:, tt, hh * 512:(hh + 1) * 512], op0=ALU.mult, op1=ALU.add))(),
                    reads=[rPS[yb + hh], rX[tt]], writes=[rX[tt]], tag="f_res")

        for tt in range(T):
            prep(tt)
            if tt >= 1:
                mms(tt - 1)
            if post_tile and tt >= 2:
                post_tile(tt - 2)
        mms(T - 1)
        if post_tile:
            post_tile(T - 2)
            post_tile(T - 1)

    lnt = alloc(stack, "lnt", [128, 4, D], BF16, side="right")
    rLnt = [Res(), Res(), Res(), Res()]
    aw_stack = contextlib.ExitStack()
    wq_sb = alloc(aw_stack, "wq_sb", [128, KC, H * DH], BF16, side="right")
    wk_sb = alloc(aw_stack, "wk_sb", [128, KC, H * DH], BF16, side="right")
    wvg_sb = alloc(aw_stack, "wvg_sb", [128, KC, H * DH + H], BF16, side="right")
    wfl = alloc(aw_stack, "wfl", [128, 4, H * DH], F32, side="right")
    rWq, rWk, rWvg = Res(), Res(), Res()
    rWfl = [Res() for _ in range(4)]

    f1_stack = contextlib.ExitStack()
    wft = alloc(f1_stack, "wft1", [128, KC, D], BF16, side="right")
    lnTt = alloc(f1_stack, "lnT1", [128, 2, KC, 128], BF16, side="right")
    wld = alloc(f1_stack, "wld1", [128, 2, D], F32, side="right")
    wsg = alloc(f1_stack, "wsg1", [128, D], BF16, side="right")
    rWft, rWld, rWsg, rLnT = [Res(), Res()], [Res(), Res()], Res(), [Res(), Res()]
    wftsp_h = nc.dram_tensor("wft_sp", [128, KC * D], BF16, kind="Internal").ap()

    build_wft(wft, rWft, wld, rWld, wsg, rWsg, "1")
    ln_stats(0)
    nwl = [0]
    for k in range(KC):
        for dst, src, rdst in ((wq_sb, wq_h, rWq), (wk_sb, wk_h, rWk), (wvg_sb, wv_h, rWvg)):
            fb = nwl[0] % 4
            nwl[0] += 1
            p.do("Q", (lambda k=k, src=src, fb=fb: lambda g: g.dma_start(
                out=wfl[:, fb, :].rearrange("pp (h e) -> pp h e", h=H),
                in_=src[:, k * 128:(k + 1) * 128, :].rearrange("h d e -> d h e")))(),
                writes=[rWfl[fb]], dma=True, tag="ldw_a")
            p.do("V", (lambda k=k, dst=dst, fb=fb: lambda v: v.tensor_scalar(
                dst[:, k, 0:H * DH], wfl[:, fb, :], rmswc[:, k:k + 1], None,
                op0=ALU.mult))(),
                reads=[rWfl[fb], rRmsw], writes=[rdst], tag="sclw_a")
        fb = nwl[0] % 4
        nwl[0] += 1
        p.do("Q", (lambda k=k, fb=fb: lambda g: g.dma_start(
            out=wfl[:, fb, 0:H], in_=wg_h[k * 128:(k + 1) * 128, :]))(),
            writes=[rWfl[fb]], dma=True, tag="ldwg")
        p.do("V", (lambda k=k, fb=fb: lambda v: v.tensor_scalar(
            wvg_sb[:, k, H * DH:H * DH + H], wfl[:, fb, 0:H], rmswc[:, k:k + 1], None,
            op0=ALU.mult))(),
            reads=[rWfl[fb], rRmsw], writes=[rWvg], tag="sclwg")

    def _a0_post(tt):
        stats_tile(8 * T, tt)
        if tt == T // 2 - 1:
            stats_math(8 * T, rms=True, lo=0, hi=T // 2)
        if tt == T - 1:
            stats_math(8 * T, rms=True, lo=T // 2, hi=T)
    ffn(0, wft, rWft, lnt, rLnt, lnTt, rLnT, post_tile=_a0_post)
    p.do("Q", lambda q: q.dma_start(
        out=wftsp_h, in_=wft.ap().rearrange("pp k dd -> pp (k dd)")),
        reads=[rWft[0], rWft[1]], dma=True, tag="wft_sp")
    f1_stack.close()
    p.fence()

    qhat = alloc(stack, "qhat", [65, H, N], BF16)
    khat = alloc(stack, "khat", [65, H, N], BF16)
    Vsb = alloc(stack, "Vsb", [128, T, H, DH + 1], BF16)
    g_tok = alloc(stack, "g_tok", [128, T, H], BF16)
    gscr = alloc(stack, "gscr", [4, 2, 128], BF16)
    p.do("V", lambda v: v.memset(
        Vsb[:, :, :, DH:DH + 1], 1.0),
        writes=[rV[t] for t in range(T)], tag="vones")
    p.do("V", lambda v: v.memset(khat[64:65, :, :], -1.0),
         writes=[rKhat[h] for h in range(H)], tag="khat1")

    a12_stack = contextlib.ExitStack()
    tTc = alloc(a12_stack, "tTc", [128, KC, 2, 512], BF16, side="right")
    rTTc = [Res(), Res()]


    def emit_tprep(sc):
        cb = sc % 2
        for i4 in range(4):
            tt = sc * 4 + i4
            bb = tt % 4
            p.do("S", (lambda tt=tt, bb=bb: lambda s: s.activation(
                lnt[:, bb, :], X[:, tt, :], AF.Identity,
                bias=stat_col(8 * T, 3, tt), scale=stat_col(8 * T, 2, tt)))(),
                reads=[rX[tt], rSt(8 * T, tt // (T // 2))],
                writes=[rLnt[bb]], tag="t_ln")
        for i4 in range(4):
            tt = sc * 4 + i4
            bb = tt % 4
            hb = tt % 2
            for k in range(KC):
                p.do("T", (lambda bb=bb, hb=hb, k=k: lambda t: t.transpose(
                    psbf[:, hb * 1024 + k * 128: hb * 1024 + (k + 1) * 128],
                    lnt[:, bb, k * 128:(k + 1) * 128], identbf[:, :]))(),
                    reads=[rLnt[bb], rConst], writes=[rPB[hb]], tag="t_tr")
            p.do("V", (lambda cb=cb, hb=hb, i4=i4: lambda v: v.tensor_copy(
                tTc[:, :, cb, i4 * 128:(i4 + 1) * 128],
                psbf[:, hb * 1024:(hb + 1) * 1024]
                .rearrange("pp (k s) -> pp k s", k=KC)))(),
                reads=[rPB[hb]], writes=[rTTc[cb]], tag="t_ev")

    def emit_qkvg(sc):
        cb = sc % 2
        for h in range(H):
            bq = (h % 2) * 2
            bk = bq + 1
            for k in range(KC):
                p.do("T", (lambda h=h, k=k, bq=bq, cb=cb: lambda t: t.matmul(
                    bank(bq)[0:64, :], wq_sb[:, k, h * DH:(h + 1) * DH],
                    tTc[:, k, cb, :], start=(k == 0), stop=(k == KC - 1)))(),
                    reads=[rWq, rTTc[cb]], writes=[rPS[bq]], tag="qk_q")
                p.do("T", (lambda h=h, k=k, bk=bk, cb=cb: lambda t: t.matmul(
                    bank(bk)[0:64, :], wk_sb[:, k, h * DH:(h + 1) * DH],
                    tTc[:, k, cb, :], start=(k == 0), stop=(k == KC - 1)))(),
                    reads=[rWk, rTTc[cb]], writes=[rPS[bk]], tag="qk_k")
            p.do("V", (lambda h=h, sc=sc, bq=bq: lambda v: v.tensor_scalar_mul(
                qhat[0:64, h, sc * 512:(sc + 1) * 512], bank(bq)[0:64, :], 0.125))(),
                reads=[rPS[bq]], writes=[rQhat[h]], tag="qk_evq")
            p.do("V", (lambda h=h, sc=sc, bk=bk: lambda v: v.tensor_copy(
                khat[0:64, h, sc * 512:(sc + 1) * 512], bank(bk)[0:64, :]))(),
                reads=[rPS[bk]], writes=[rKhat[h]], tag="qk_evk")
        for i4 in range(4):
            tt = sc * 4 + i4
            bv = 4 + (i4 % 2)
            for k in range(KC):
                p.do("T", (lambda k=k, bv=bv, cb=cb, i4=i4: lambda t: t.matmul(
                    bank(bv)[:, :], tTc[:, k, cb, i4 * 128:(i4 + 1) * 128],
                    wvg_sb[:, k, 0:H * DH], start=(k == 0), stop=(k == KC - 1)))(),
                    reads=[rTTc[cb], rWvg], writes=[rPS[bv]], tag="v_mm")
                p.do("T", (lambda k=k, cb=cb, i4=i4: lambda t: t.matmul(
                    bank(0)[:, 0:H], tTc[:, k, cb, i4 * 128:(i4 + 1) * 128],
                    wvg_sb[:, k, H * DH:H * DH + H], start=(k == 0), stop=(k == KC - 1)))(),
                    reads=[rTTc[cb], rWvg], writes=[rPS[0]], tag="g_mm")
            p.do("V", (lambda tt=tt, bv=bv: lambda v: v.tensor_copy(
                Vsb[:, tt, :, 0:DH],
                bank(bv).rearrange("pp (h e) -> pp h e", h=H)))(),
                reads=[rPS[bv]], writes=[rV[tt]], tag="v_ev")
            p.do("S", (lambda tt=tt: lambda s: s.activation(
                g_tok[:, tt, :], bank(0)[:, 0:H], AF.Sigmoid))(),
                reads=[rPS[0]], writes=[rGtok], tag="g_ev")

    for sc in range(IC):
        emit_tprep(sc)
        if sc >= 1:
            emit_qkvg(sc - 1)
    emit_qkvg(IC - 1)
    a12_stack.close()
    aw_stack.close()
    p.fence(exclude=("V",))

    a3_stack = contextlib.ExitStack()
    wo_bf = nc.dram_tensor("wo_bf", [H, DH, D], BF16, kind="Internal").ap()
    wobuf = alloc(a3_stack, "wobuf", [64, 4, D], BF16, side="right")
    outev = alloc(a3_stack, "outev", [64, 2, 512], F32, side="right")
    outs = alloc(a3_stack, "outs", [64, H, 512], BF16, side="right")
    Lrow = alloc(a3_stack, "Lrow", [65, 2, 512], F32, side="right")
    gRow = alloc(a3_stack, "gRow", [65, 2, 512], BF16, side="right")
    c2row = alloc(a3_stack, "c2row", [65, 2, 512], F32, side="right")
    c2bf = alloc(a3_stack, "c2bf", [65, 2, 512], BF16, side="right")
    c2full = alloc(a3_stack, "c2full", [64, 2, 512], BF16, side="right")
    ones64 = alloc(a3_stack, "ones64", [65, 64], BF16, side="right")
    lnmk = alloc(a3_stack, "lnmk", [1, H], F32, side="right")
    bnd_stack = contextlib.ExitStack()
    qks = alloc(bnd_stack, "qks", [64, 2, N], BF16, side="right")
    rowA = alloc(bnd_stack, "rowA", [1, 512], F32, side="right")
    pmk = alloc(bnd_stack, "pmk", [1, 8], F32, side="right")
    rWo = Res()
    rWobuf = [Res() for _ in range(4)]
    rPt = [Res() for _ in range(4)]
    rC2 = Res()
    rC2full = [Res(), Res()]
    p.do("V", lambda v: v.memset(ones64[:, :], 1.0), writes=[rConst], tag="ones64")
    rOutev = [Res(), Res()]
    rOuts = Res()
    rLrow = [Res(), Res()]
    rGrow = [Res(), Res()]
    rQks = Res()
    rRowA = Res()
    rPmk = Res()
    rLnmk = Res()

    p.do("G", lambda g: g.dma_start(
        out=wo_bf.rearrange("h e d -> (h e) d"),
        in_=wo_h.rearrange("h e d -> (h e) d")),
        writes=[rWo], dma=True, tag="wo_cast")

    ringc = [0]

    def ring1():
        b = ringc[0] % 4
        ringc[0] += 1
        return b

    def ring2():
        if ringc[0] % 2:
            ringc[0] += 1
        b = ringc[0] % 4
        ringc[0] += 2
        return b

    def emit_bound(h):
        ext = ([rTTc[0], rTTc[1], rWq, rWk, rWvg] + rWfl) if h == 0 else []
        p.do("V", (lambda h=h: lambda v: v.tensor_mul(
            qks[:, 0, :], qhat[0:64, h, :], qhat[0:64, h, :]))(),
            reads=[rQhat[h]] + ext, writes=[rQks], tag="b_qsq")
        p.do("V", (lambda h=h: lambda v: v.tensor_mul(
            qks[:, 1, :], khat[0:64, h, :], khat[0:64, h, :]))(),
            reads=[rKhat[h]], writes=[rQks], tag="b_ksq")
        kslots = []
        for icc in range(IC):
            b = ring1()
            kslots.append(b)
            p.do("T", (lambda icc=icc, b=b: lambda t: t.matmul(
                bank(b)[0:1, :], onesc[0:64, 0:1], qks[:, 1, icc * 512:(icc + 1) * 512],
                start=True, stop=True))(),
                reads=[rQks, rConst], writes=[rPS[b]], tag="b_kmm")
        for icc, b in enumerate(kslots):
            p.do("V", (lambda icc=icc, b=b: lambda v: v.reduce_max(
                pmk[0:1, icc:icc + 1], bank(b)[0:1, :], axis=AX.X))(),
                reads=[rPS[b]], writes=[rPmk], tag="b_kpm")
        p.do("V", (lambda: lambda v: v.reduce_max(
            pmk[0:1, IC:IC + 1], pmk[0:1, 0:IC], axis=AX.X))(),
            reads=[rPmk], writes=[rPmk], tag="b_kmx")
        p.do("S", (lambda h=h: lambda s: s.activation(
            pmk[0:1, IC + 1:IC + 2], pmk[0:1, IC:IC + 1], AF.Ln, bias=1e-30))(),
            reads=[rPmk], writes=[rPmk], tag="b_lnk")
        p.do("V", (lambda h=h: lambda v: v.tensor_scalar_mul(
            lnmk[0:1, h:h + 1], pmk[0:1, IC + 1:IC + 2], 0.5))(),
            reads=[rPmk], writes=[rLnmk], tag="b_lnk2")
        for icc in range(IC):
            b = ring1()
            p.do("T", (lambda icc=icc, b=b: lambda t: t.matmul(
                bank(b)[0:1, :], onesc[0:64, 0:1], qks[:, 0, icc * 512:(icc + 1) * 512],
                start=True, stop=True))(),
                reads=[rQks, rConst], writes=[rPS[b]], tag="b_qmm")
            p.do("S", (lambda icc=icc, b=b: lambda s: s.activation(
                rowA[0:1, :], bank(b)[0:1, :], AF.Ln, bias=1e-30))(),
                reads=[rPS[b]], writes=[rRowA], tag="b_lnq")
            p.do("S", (lambda h=h, icc=icc: lambda s: s.activation(
                qhat[64:65, h, icc * 512:(icc + 1) * 512], rowA[0:1, :], AF.Exp,
                scale=0.5, bias=lnmk[0:1, h:h + 1]))(),
                reads=[rRowA, rLnmk], writes=[rQhat[h]], tag="b_c")

    pending_gate = [None]
    pending_gate_t = [None]
    pend_pv = [None]
    pend_epi = [None]

    def flush_gate_t():
        if pending_gate_t[0] is not None:
            pending_gate_t[0]()
            pending_gate_t[0] = None

    def flush_gate():
        flush_gate_t()
        if pending_gate[0] is not None:
            pending_gate[0]()
            pending_gate[0] = None

    def flush_tail():
        if pend_pv[0] is not None:
            pend_pv[0]()
            pend_pv[0] = None
        if pend_epi[0] is not None:
            pend_epi[0]()
            pend_epi[0] = None

    def emit_pass2(ic, h):
        ob = h % 2
        ab = 4 + (h % 2)

        def emit_grow(h=h, ic=ic, ob=ob):
            gp = h % 2
            gof = gp * 1024
            p.do("T", (lambda h=h, ic=ic, gof=gof: lambda t: t.transpose(
                psbf[0:4, gof:gof + 128], g_tok[:, 4 * ic:4 * ic + 4, h],
                identbf[:, :]))(),
                reads=[rGtok, rConst], writes=[rPB[gp]], tag="g_tr")
            p.do("V", (lambda gp=gp, gof=gof: lambda v: v.tensor_copy(
                gscr[:, gp, :], psbf[0:4, gof:gof + 128]))(),
                reads=[rPB[gp]], writes=[rGscr], tag="g_scr")
            p.do("Q", (lambda ob=ob, gp=gp: lambda q: q.dma_start(
                out=gRow[64:65, ob, :].rearrange("o (t f) -> o t f", t=4),
                in_=gscr[:, gp, :]))(),
                reads=[rGscr], writes=[rGrow[ob]], dma=True, chain=0, tag="g_dma")
        npair = 2 * (ic + 1)

        def emit_pv(pg, pb):
            for b in range(2):
                jt = 2 * pg + b
                p.do("T", (lambda h=h, jt=jt, b=b, pb=pb, ab=ab,
                           first=(pg == 0 and b == 0),
                           last=(pg == npair - 1 and b == 1): lambda t: t.matmul(
                    bank(ab)[0:65, :], Vsb[:, jt, h, :],
                    pt[:, pb, b * 512:(b + 1) * 512],
                    start=first, stop=last))(),
                    reads=[rPt[pb], rV[jt]], writes=[rPS[ab]], tag="pv_mm")

        def epilogue(h=h, ob=ob, ab=ab):
            p.do("V", (lambda ob=ob, ab=ab: lambda v: v.tensor_copy(
                outev[:, ob, :], bank(ab)[0:64, :]))(),
                reads=[rPS[ab]], writes=[rOutev[ob]], tag="pv_ev")
            if h == H - 1:
                flush_gate()
                p.do("S", (lambda ob=ob, ab=ab: lambda s: s.activation(
                    c2row[64:65, ob, :], bank(ab)[64:65, :], AF.Ln))(),
                    reads=[rPS[ab]], writes=[rC2], tag="c2ln_f")
                p.do("S", (lambda ob=ob: lambda s: s.activation(
                    c2row[64:65, ob, :], c2row[64:65, ob, :], AF.Exp, scale=-1.0))(),
                    reads=[rC2], writes=[rC2], tag="c2exp_f")
                p.do("V", (lambda ob=ob: lambda v: v.tensor_mul(
                    c2bf[64:65, ob, :], c2row[64:65, ob, :], gRow[64:65, ob, :]))(),
                    reads=[rGrow[ob]], writes=[rC2], tag="c2_f")
                cb_b = ring1()
                p.do("T", (lambda ob=ob, cb_b=cb_b: lambda t: t.matmul(
                    bank(cb_b)[0:64, :], ones64[64:65, 0:64], c2bf[64:65, ob, :],
                    start=True, stop=True))(),
                    reads=[rC2, rConst], writes=[rPS[cb_b]], tag="cb_mm_f")
                p.do("V", (lambda h=h, ob=ob, cb_b=cb_b: lambda v: v.tensor_tensor(
                    outs[:, h, :], outev[:, ob, :], bank(cb_b)[0:64, :],
                    op=ALU.mult))(),
                    reads=[rOutev[ob], rPS[cb_b]], writes=[rOuts], tag="c_scale_f")
                pending_gate[0] = None
                return
            p.do("V", (lambda ob=ob, ab=ab: lambda v: v.tensor_copy(
                Lrow[64:65, ob, :], bank(ab)[64:65, :]))(),
                reads=[rPS[ab]], writes=[rLrow[ob]], tag="l_ev")

            def gate(h=h, ob=ob):
                p.do("S", (lambda ob=ob: lambda s: s.activation(
                    c2row[64:65, ob, :], Lrow[64:65, ob, :], AF.Ln))(),
                    reads=[rLrow[ob]], writes=[rC2], tag="c2ln")
                p.do("S", (lambda ob=ob: lambda s: s.activation(
                    c2row[64:65, ob, :], c2row[64:65, ob, :], AF.Exp, scale=-1.0))(),
                    reads=[rC2], writes=[rC2], tag="c2exp")
                p.do("V", (lambda ob=ob: lambda v: v.tensor_mul(
                    c2bf[64:65, ob, :], c2row[64:65, ob, :], gRow[64:65, ob, :]))(),
                    reads=[rGrow[ob]], writes=[rC2], tag="c2")

                def _bc(ob=ob):
                    def run(q):
                        s = c2bf[64:65, ob, :]
                        bsrc = AP(s.tensor, s.offset,
                                  [(s.ap[0][0], 1), (0, 64)] + list(s.ap[1:]))
                        return q.dma_start(out=c2full[:, ob, :], in_=bsrc)
                    return run
                p.do("Q", _bc(), reads=[rC2], writes=[rC2full[ob]],
                     dma=True, chain=1, tag="c2_bc")

                def gate_t(h=h, ob=ob):
                    p.do("V", (lambda h=h, ob=ob: lambda v: v.tensor_tensor(
                        outs[:, h, :], outev[:, ob, :], c2full[:, ob, :],
                        op=ALU.mult))(),
                        reads=[rOutev[ob], rC2full[ob]], writes=[rOuts], tag="c_scale")
                pending_gate_t[0] = gate_t
            flush_gate()
            pending_gate[0] = gate

        lpv = []
        for pg in range(npair):
            pb = pg % 4
            if pg == 1:
                flush_tail()
                emit_grow()
            b0 = ring2()
            for b in range(2):
                jt = 2 * pg + b
                p.do("T", (lambda h=h, ic=ic, jt=jt, b0=b0, b=b:
                           lambda t: t.matmul(
                    bank(b0 + b)[:, :], khat[:, h, jt * 128:(jt + 1) * 128],
                    qhat[:, h, ic * 512:(ic + 1) * 512],
                    start=True, stop=True))(),
                    reads=[rQhat[h], rKhat[h]], writes=[rPS[b0 + b]], tag="s2_mm")
            p.do("S", (lambda pb=pb, b0=b0: lambda s: s.activation(
                pt[:, pb, :], banks(b0, 2), AF.Exp))(),
                reads=[rPS[b0], rPS[b0 + 1]], writes=[rPt[pb]], tag="s2_exp")
            for b in range(2):
                jt = 2 * pg + b
                r = jt - 4 * ic
                if r >= 0:
                    if r > 0:
                        p.do("V", (lambda pb=pb, b=b, r=r: lambda v: v.memset(
                            pt[:, pb, b * 512: b * 512 + r * 128], 0.0))(),
                            writes=[rPt[pb]], tag="z_slab")
                    p.do("V", (lambda pb=pb, b=b, r=r: lambda v: v.tensor_mul(
                        pt[:, pb, b * 512 + r * 128: b * 512 + (r + 1) * 128],
                        pt[:, pb, b * 512 + r * 128: b * 512 + (r + 1) * 128],
                        ubin[:, :]))(),
                        reads=[rConst], writes=[rPt[pb]], tag="z_tri")
            if pg == 0:
                flush_gate_t()
            lpv.append((pg, pb))
            if len(lpv) > 2:
                emit_pv(*lpv.pop(0))
        if len(lpv) > 1:
            emit_pv(*lpv.pop(0))
        pend_pv[0] = (lambda a=lpv.pop(0): lambda: emit_pv(*a))()
        pend_epi[0] = epilogue

    def emit_wo(ic):
        flush_gate_t()
        flush_tail()
        flush_gate()
        flush_gate_t()
        for ph in range(2):
            for h in range(H):
                wb = h % 4
                p.do("Q", (lambda h=h, wb=wb: lambda q: q.dma_start(
                    out=wobuf[:, wb, :], in_=wo_bf[h, :, :]))(),
                    reads=[rWo], writes=[rWobuf[wb]], dma=True, chain=2 + (h % 2),
                    tag="ldwo")
                for i2 in range(2):
                    pb0 = i2 * 2
                    itr = ph * 2 + i2
                    for hh in range(2):
                        p.do("T", (lambda h=h, wb=wb, itr=itr, hh=hh, pb0=pb0:
                                   lambda t: t.matmul(
                            bank(pb0 + hh)[:, :], outs[:, h, itr * 128:(itr + 1) * 128],
                            wobuf[:, wb, hh * 512:(hh + 1) * 512],
                            start=(h == 0), stop=(h == H - 1)))(),
                            reads=[rOuts, rWobuf[wb]], writes=[rPS[pb0 + hh]], tag="wo_mm")
            for i2 in range(2):
                itr = ph * 2 + i2
                tt = 4 * ic + itr
                pb0 = i2 * 2
                for hh in range(2):
                    p.do("V", (lambda tt=tt, hh=hh, pb0=pb0: lambda v: v.tensor_add(
                        X[:, tt, hh * 512:(hh + 1) * 512],
                        X[:, tt, hh * 512:(hh + 1) * 512], bank(pb0 + hh)[:, :]))(),
                        reads=[rPS[pb0 + hh], rX[tt]], writes=[rX[tt]], tag="wo_res")
                stats_tile(16 * T, tt)

    for h in range(H):
        emit_bound(h)
    bnd_stack.close()
    pt = alloc(a3_stack, "pt", [128, 4, 1024], BF16, side="right")
    for ic in range(IC):
        for h in range(H):
            emit_pass2(ic, h)
        emit_wo(ic)
    a3_stack.close()
    p.fence()

    stats_math(16 * T, lo=0, hi=T // 2)
    stats_math(16 * T, lo=T // 2, hi=T)

    f2_stack = contextlib.ExitStack()
    wft2 = alloc(f2_stack, "wft2", [128, KC, D], BF16, side="right")
    lnTt2 = alloc(f2_stack, "lnT2", [128, 2, KC, 128], BF16, side="right")
    rWft2, rLnT2 = [Res(), Res()], [Res(), Res()]
    p.do("Q", lambda q: q.dma_start(
        out=wft2.ap().rearrange("pp k dd -> pp (k dd)"), in_=wftsp_h),
        writes=[rWft2[0], rWft2[1]], dma=True, tag="wft_rl")
    ffn(16 * T, wft2, rWft2, lnt, rLnt, lnTt2, rLnT2)

    for tt in range(T):
        p.do("Q", (lambda tt=tt: lambda q: q.dma_start(
            out=out_h.rearrange("(t pp) d -> pp t d", pp=128)[:, tt, :],
            in_=X[:, tt, :]))(),
            reads=[rX[tt]], dma=True, tag="stX")
    f2_stack.close()

    totals = p.finalize()

    sem_keys = ["T", "V", "S"] + [f"{e}{c}" for e in ("G", "Q") for c in range(Prog.NCHAIN)]
    sems = {k: stack.enter_context(nc.semaphore("sem" + k)) for k in sem_keys}

    with nc.Block() as block:
        def make_body(ename):
            def body(eng):
                for op in p.ops[ename]:
                    if ename not in ("G", "Q") and op.drain_before:
                        eng.drain()
                    for pe, v in op.waits:
                        eng.wait_ge(sems[pe], v)
                    inst = op.emit(eng)
                    if op.need_inc:
                        inst.then_inc(sems[op.sem_key], 16 if op.dma else 1)
                if ename == "Q":
                    for c in range(Prog.NCHAIN):
                        if totals.get(f"Q{c}", 0):
                            eng.wait_ge(sems[f"Q{c}"], totals[f"Q{c}"])
                        if totals.get(f"G{c}", 0):
                            eng.wait_ge(sems[f"G{c}"], totals[f"G{c}"])
            return body
        block.tensor(make_body("T"))
        block.vector(make_body("V"))
        block.scalar(make_body("S"))
        block.gpsimd(make_body("G"))
        block.sync(make_body("Q"))

    stack.close()
    return nc


def _consts():
    ident = np.eye(128)
    f = np.arange(128)[None, :]
    pp = np.arange(128)[:, None]
    consts = {
        "identbf": ident.astype(BF),
        "identf": ident.astype(np.float32),
        "umask": np.where(f > pp, NEGBIG, 0.0).astype(BF),
        "ubin": (f >= pp).astype(np.float32).astype(BF),
    }
    return consts


def kernel(L0_attn_tokens, L0_attn_wq, L0_attn_wk, L0_attn_wv, L0_attn_wo,
           L0_attn_w_lr, L0_attn_w_target, L0_attn_w_gates, L0_attn_rms_w,
           L0_ffn_weight):
    B, N, D_ = L0_attn_tokens.shape
    assert D_ == D
    nc = build_program(N)
    common = {
        "wq": np.ascontiguousarray(L0_attn_wq, np.float32),
        "wk": np.ascontiguousarray(L0_attn_wk, np.float32),
        "wv": np.ascontiguousarray(L0_attn_wv, np.float32),
        "wo": np.ascontiguousarray(L0_attn_wo, np.float32),
        "wg": np.ascontiguousarray(L0_attn_w_gates, np.float32),
        "rmsw": np.ascontiguousarray(np.asarray(L0_attn_rms_w, np.float32).reshape(KC, 128).T),
        "wf": np.ascontiguousarray(L0_ffn_weight, np.float32),
    }
    common.update(_consts())
    in_maps = []
    for c in range(B):
        m = dict(common)
        m["x"] = np.ascontiguousarray(L0_attn_tokens[c], np.float32)
        in_maps.append(m)
    res = run_bass_kernel_spmd(nc, in_maps, core_ids=list(range(B)))
    return np.stack([res.results[c]["out"] for c in range(B)], axis=0)


# version 46 (final)
# speedup vs baseline: 1.0201x; 1.0031x over previous
import contextlib
import numpy as np
import ml_dtypes

import concourse.bass as bass
import concourse.mybir as mybir
from concourse.bass_utils import run_bass_kernel_spmd
from concourse.bass_types import AP

F32 = mybir.dt.float32
BF16 = mybir.dt.bfloat16
AF = mybir.ActivationFunctionType
ALU = mybir.AluOpType
AX = mybir.AxisListType

D = 1024
H = 8
DH = 64
KC = 8
NEGBIG = -3.0e38
EPS_LN = 1e-5
EPS_RMS = float(np.finfo(np.float32).eps)
BF = ml_dtypes.bfloat16


class Op:
    __slots__ = ("eng", "emit", "deps", "dma", "need_inc", "inc_val", "waits", "tag", "sem_key", "idx", "drain_before", "chain")

    def __init__(self, eng, emit, deps, dma, tag):
        self.eng = eng
        self.emit = emit
        self.deps = deps
        self.dma = dma
        self.need_inc = dma
        self.inc_val = 0
        self.waits = None
        self.tag = tag
        self.drain_before = False
        self.chain = None


class Res:
    __slots__ = ("writers", "readers")

    def __init__(self):
        self.writers = []
        self.readers = []

    def rdeps(self):
        return [(w, True) for w in self.writers]

    def wdeps(self):
        return [(r, True) for r in self.readers] + [(w, False) for w in self.writers]

    def note_read(self, op):
        self.readers.append(op)

    def note_write(self, op):
        if self.readers:
            self.writers = [op]
            self.readers = []
        else:
            self.writers.append(op)


class Prog:
    ENGS = ("T", "V", "S", "G", "Q")

    def __init__(self, race_safe=False):
        self.ops = {e: [] for e in self.ENGS}
        self.fence_deps = []
        self.fence_exclude = ()
        self.auto_deps = {}
        self.race_safe = race_safe

    def do(self, eng, emit, reads=(), writes=(), dma=False, tag="", chain=None):
        deps = [] if eng in self.fence_exclude else \
            [(d, True) for d in self.fence_deps]
        for r in self.auto_deps.get(eng, ()):
            deps += r.rdeps()
        for r in reads:
            deps += r.rdeps()
        for w in writes:
            deps += w.wdeps()
        op = Op(eng, emit, deps, dma, tag)
        op.chain = chain
        op.idx = len(self.ops[eng])
        self.ops[eng].append(op)
        for r in reads:
            r.note_read(op)
        for w in writes:
            w.note_write(op)
        return op

    def fence(self, exclude=()):
        self.fence_deps = [self.ops[e][-1] for e in self.ENGS if self.ops[e]]
        self.fence_exclude = exclude


    NCHAIN = 6

    def finalize(self):
        for e in self.ENGS:
            for op in self.ops[e]:
                for d, _hz in op.deps:
                    if d.eng != op.eng or d.dma:
                        d.need_inc = True
        totals = {}
        for e in self.ENGS:
            if e in ("G", "Q"):
                nd = 0
                chain_vals = [0] * self.NCHAIN
                for op in self.ops[e]:
                    if op.chain is not None:
                        c = op.chain % self.NCHAIN
                    else:
                        c = nd % self.NCHAIN
                        nd += 1
                    chain_vals[c] += 16
                    op.sem_key = f"{e}{c}"
                    op.inc_val = chain_vals[c]
                for c in range(self.NCHAIN):
                    totals[f"{e}{c}"] = chain_vals[c]
            else:
                cnt = 0
                for op in self.ops[e]:
                    if op.need_inc:
                        cnt += 1
                    op.sem_key = e
                    op.inc_val = cnt
                totals[e] = cnt
        for e in self.ENGS:
            hwm = {}
            for op in self.ops[e]:
                w = {}
                if op.dma:
                    if op.inc_val > 16:
                        w[op.sem_key] = op.inc_val - 16
                for dd, _hz in op.deps:
                    if dd.eng == op.eng and not dd.dma:
                        continue
                    if dd.inc_val > w.get(dd.sem_key, 0):
                        w[dd.sem_key] = dd.inc_val
                op.waits = [(pe, v) for pe, v in w.items() if v > hwm.get(pe, 0)]
                for pe, v in op.waits:
                    hwm[pe] = v
        drain_engines = ("V", "S") if self.race_safe else ("V",)
        for e in self.ENGS:
            if e not in drain_engines:
                continue
            drained_to = -1
            for op in self.ops[e]:
                need = -1
                for dd, hz in op.deps:
                    if (hz or self.race_safe) and dd.eng == e and not dd.dma \
                            and dd.idx > need:
                        need = dd.idx
                if need > drained_to:
                    op.drain_before = True
                    drained_to = op.idx - 1
                else:
                    op.drain_before = False
        return totals


def build_program(N, race_safe=False):
    T = N // 128
    IC = N // 512
    assert IC * 512 == N and T % 4 == 0

    nc = bass.Bass("TRN2", target_bir_lowering=False, debug=False,
                   detect_race_conditions=race_safe)

    x_h = nc.dram_tensor("x", [N, D], F32, kind="ExternalInput").ap()
    wq_h = nc.dram_tensor("wq", [H, D, DH], F32, kind="ExternalInput").ap()
    wk_h = nc.dram_tensor("wk", [H, D, DH], F32, kind="ExternalInput").ap()
    wv_h = nc.dram_tensor("wv", [H, D, DH], F32, kind="ExternalInput").ap()
    wo_h = nc.dram_tensor("wo", [H, DH, D], F32, kind="ExternalInput").ap()
    wg_h = nc.dram_tensor("wg", [D, H], F32, kind="ExternalInput").ap()
    rmsw_h = nc.dram_tensor("rmsw", [128, KC], F32, kind="ExternalInput").ap()
    wf_h = nc.dram_tensor("wf", [D, D], F32, kind="ExternalInput").ap()
    identbf_h = nc.dram_tensor("identbf", [128, 128], BF16, kind="ExternalInput").ap()
    identf_h = nc.dram_tensor("identf", [128, 128], F32, kind="ExternalInput").ap()
    umask_h = nc.dram_tensor("umask", [128, 128], BF16, kind="ExternalInput").ap()
    ubin_h = nc.dram_tensor("ubin", [128, 128], BF16, kind="ExternalInput").ap()
    out_h = nc.dram_tensor("out", [N, D], F32, kind="ExternalOutput").ap()

    p = Prog(race_safe)
    stack = contextlib.ExitStack()

    def alloc(st, name, shape, dtype, side="left"):
        return st.enter_context(nc.sbuf_tensor(name, shape, dtype, side=side))

    X = alloc(stack, "X", [128, T, D], F32)
    identbf = alloc(stack, "identbf_s", [128, 128], BF16)
    identf = alloc(stack, "identf_s", [128, 128], F32)
    umask = alloc(stack, "umask_s", [128, 128], BF16)
    ubin = alloc(stack, "ubin_s", [128, 128], BF16)
    stat = alloc(stack, "stat", [128, 24 * T], F32)
    rmswc = alloc(stack, "rmswc", [128, KC], F32)
    sq_scr = alloc(stack, "sq_scr", [128, D], BF16)
    ones1 = alloc(stack, "ones1", [1, 128], BF16)
    onesc = alloc(stack, "onesc", [65, 1], BF16)
    CONST_VALS = [EPS_LN, EPS_RMS, 0.125, 0.5, -1.0, 1.0 / D, 1.0, 0.0, 1e-30]
    cdb = alloc(stack, "cdb", [128, len(CONST_VALS)], F32)
    for ci, cv in enumerate(CONST_VALS):
        nc.const_aps.register(F32, cv, cdb[:, ci:ci + 1])
    epsr = alloc(stack, "epsr", [128, 1], F32)

    ps8 = stack.enter_context(nc.psum_tensor("ps8", [128, 3072], F32))
    psbf = stack.enter_context(nc.psum_tensor("psbf", [128, 2048], BF16))

    rX = [Res() for _ in range(T)]
    rPS = [Res() for _ in range(6)]
    rPB = [Res() for _ in range(2)]
    rStatK = {}

    def rSt(base, half):
        return rStatK.setdefault((base, half), Res())
    rStSum = Res()
    rStSq = Res()
    rQhat = [Res() for _ in range(H)]
    rKhat = [Res() for _ in range(H)]
    rV = [Res() for _ in range(T)]
    rGtok = Res()
    rLscr = Res()
    rGscr = Res()
    rRmsw = Res()
    rConst = Res()
    rSq = Res()

    def bank(b):
        return ps8[:, b * 512:(b + 1) * 512]

    def banks(b0, nb):
        return ps8[:, b0 * 512:(b0 + nb) * 512]

    for dst, src in ((identbf, identbf_h), (identf, identf_h), (umask, umask_h),
                     (ubin, ubin_h)):
        p.do("Q", (lambda dst=dst, src=src: lambda q: q.dma_start(
            out=dst[:, :], in_=src[:, :]))(), writes=[rConst], dma=True, tag="ldc")
    xstep = max(2, T // 8)
    for tt in range(0, T, xstep):
        p.do("Q", (lambda tt=tt, xstep=xstep: lambda q: q.dma_start(
            out=X[:, tt:tt + xstep, :],
            in_=x_h.rearrange("(t pp) d -> pp t d", pp=128)[:, tt:tt + xstep, :]))(),
            writes=[rX[tt + i] for i in range(xstep)], dma=True, tag="ldX")
    p.do("Q", lambda q: q.dma_start(
        out=rmswc[:, :], in_=rmsw_h[:, :]),
        writes=[rRmsw], dma=True, tag="ldrmsw")
    rCAp = Res()
    for ci, cv in enumerate(CONST_VALS):
        p.do("V", (lambda ci=ci, cv=cv: lambda v: v.memset(cdb[:, ci:ci + 1], cv))(),
             writes=[rCAp], tag="cdb")
    p.auto_deps = {"S": [rCAp], "V": [rCAp], "T": [rCAp]}
    p.do("V", lambda v: v.memset(ones1[:, :], 1.0), writes=[rConst], tag="ones1")
    p.do("V", lambda v: v.memset(onesc[:, :], 1.0), writes=[rConst], tag="onesc")

    rEps = Res()
    one_c = cdb[:, 6:7]
    p.do("V", lambda v: v.memset(epsr[:, :], 3.0), writes=[rEps], tag="eps0")
    p.do("V", lambda v: v.reciprocal(epsr[:, :], epsr[:, :]),
         reads=[rEps], writes=[rEps], tag="eps1")
    p.do("V", lambda v: v.tensor_scalar_mul(epsr[:, :], epsr[:, :], 4.0),
         reads=[rEps], writes=[rEps], tag="eps2")
    p.do("V", lambda v: v.scalar_tensor_tensor(
        epsr[:, :], one_c, -1.0, epsr[:, :], op0=ALU.mult, op1=ALU.add),
        reads=[rEps], writes=[rEps], tag="eps3")
    p.do("V", lambda v: v.tensor_scalar_mul(epsr[:, :], epsr[:, :], 3.0),
         reads=[rEps], writes=[rEps], tag="eps4")
    p.do("V", lambda v: v.scalar_tensor_tensor(
        epsr[:, :], one_c, -1.0, epsr[:, :], op0=ALU.mult, op1=ALU.add),
        reads=[rEps], writes=[rEps], tag="eps5")
    p.do("V", lambda v: v.tensor_mul(epsr[:, :], epsr[:, :], epsr[:, :]),
        reads=[rEps], writes=[rEps], tag="eps6")
    p.do("S", lambda s: s.activation(epsr[:, :], epsr[:, :], AF.Sqrt),
         reads=[rEps], writes=[rEps], tag="eps7")

    def build_wft(wft, rWft, wld, rWld, wsg, rWsg, tagp):
        for oc in range(KC):
            wb = oc % 2
            p.do("G", (lambda oc=oc, wb=wb: lambda g: g.dma_start(
                out=wld[:, wb, :], in_=wf_h[oc * 128:(oc + 1) * 128, :]))(),
                writes=[rWld[wb]], dma=True, tag="ldW" + tagp)
            p.do("S", (lambda wb=wb: lambda s: s.sign(wsg[:, :], wld[:, wb, :]))(),
                 reads=[rWld[wb]], writes=[rWsg], tag="sgnW" + tagp)
            pb = oc % 2
            for k in range(KC):
                p.do("T", (lambda oc=oc, k=k, pb=pb: lambda t: t.transpose(
                    psbf[:, pb * 1024 + k * 128: pb * 1024 + (k + 1) * 128],
                    wsg[:, k * 128:(k + 1) * 128], identbf[:, :]))(),
                    reads=[rWsg, rConst], writes=[rPB[pb]], tag="trW" + tagp)
            p.do("V", (lambda oc=oc, pb=pb: lambda v: v.tensor_copy(
                wft[:, :, oc * 128:(oc + 1) * 128], psbf[:, pb * 1024:(pb + 1) * 1024]
                .rearrange("pp (k s) -> pp k s", k=KC)))(),
                reads=[rPB[pb]], writes=[rWft[oc // 4]], tag="evW" + tagp)

    def stats_tile(base, tt):
        p.do("V", (lambda base=base, tt=tt: lambda v: v.reduce_sum(
            stat[:, base + 4 * T + tt: base + 4 * T + tt + 1], X[:, tt, :],
            axis=AX.X))(),
            reads=[rX[tt]], writes=[rStSum], tag="lnsum")
        p.do("S", (lambda base=base, tt=tt: lambda s: s.activation(
            sq_scr[:, :], X[:, tt, :], AF.Square,
            accum_out=stat[:, base + 5 * T + tt: base + 5 * T + tt + 1]))(),
            reads=[rX[tt]], writes=[rStSq, rSq], tag="lnsq")

    def stats_math(base, rms=False, lo=0, hi=None):
        if hi is None:
            hi = T
        rStat = rSt(base, 0 if lo == 0 else 1)
        ST = lambda j: stat[:, base + j * T + lo: base + j * T + hi]
        p.do("V", lambda v: v.tensor_scalar_mul(ST(0), ST(4), 1.0 / D),
             reads=[rStSum], writes=[rStat], tag="ln_m")
        p.do("V", lambda v: v.tensor_scalar_mul(ST(5), ST(5), 1.0 / D),
             reads=[rStSq], writes=[rStat], tag="ln_e2")
        p.do("V", lambda v: v.scalar_tensor_tensor(
            ST(6), ST(0), -1.0, ST(0), op0=ALU.mult, op1=ALU.mult),
            reads=[rStat], writes=[rStat], tag="ln_mm")
        p.do("V", lambda v: v.tensor_add(ST(6), ST(6), ST(5)),
             reads=[rStat], writes=[rStat], tag="ln_var")
        p.do("S", lambda s: s.activation(ST(7), ST(6), AF.Sqrt, bias=EPS_LN),
             reads=[rStat], writes=[rStat], tag="ln_sd")
        p.do("V", lambda v: v.reciprocal(ST(1), ST(7)),
             reads=[rStat], writes=[rStat], tag="ln_r")
        if rms:
            p.do("S", lambda s: s.activation(ST(7), ST(6), AF.Copy, bias=EPS_LN),
                 reads=[rStat], writes=[rStat], tag="rms_vpe")
            p.do("V", lambda v: v.reciprocal(ST(7), ST(7)),
                 reads=[rStat], writes=[rStat], tag="rms_rec")
            p.do("V", lambda v: v.tensor_mul(ST(7), ST(6), ST(7)),
                 reads=[rStat], writes=[rStat], tag="rms_E")
            p.do("S", lambda s: s.activation(ST(7), ST(7), AF.Sqrt, bias=epsr[:, 0:1]),
                 reads=[rStat, rEps], writes=[rStat], tag="rms_sq")
            p.do("V", lambda v: v.reciprocal(ST(7), ST(7)),
                 reads=[rStat], writes=[rStat], tag="rms_rec2")
            p.do("V", lambda v: v.tensor_mul(ST(2), ST(1), ST(7)),
                 reads=[rStat], writes=[rStat], tag="rms_rp")
            p.do("V", lambda v: v.scalar_tensor_tensor(
                ST(3), ST(0), -1.0, ST(2), op0=ALU.mult, op1=ALU.mult),
                reads=[rStat], writes=[rStat], tag="ln_nmr")
        else:
            p.do("V", lambda v: v.scalar_tensor_tensor(
                ST(3), ST(0), -1.0, ST(1), op0=ALU.mult, op1=ALU.mult),
                reads=[rStat], writes=[rStat], tag="ln_nmr")

    def ln_stats(base, rms=False):
        for tt in range(T):
            stats_tile(base, tt)
            if tt == T // 2 - 1:
                stats_math(base, rms, 0, T // 2)
        stats_math(base, rms, T // 2, T)

    def stat_col(base, j, tt):
        return stat[:, base + j * T + tt: base + j * T + tt + 1]

    def ffn(base, wft, rWft, lnt, rLnt, lnTt, rLnT, post_tile=None):
        def prep(tt):
            bb = tt % 2
            p.do("S", (lambda tt=tt, bb=bb: lambda s: s.activation(
                lnt[:, bb, :], X[:, tt, :], AF.Identity,
                bias=stat_col(base, 3, tt), scale=stat_col(base, 1, tt)))(),
                reads=[rX[tt], rSt(base, tt // (T // 2))],
                writes=[rLnt[bb]], tag="f_ln")
            for k in range(KC):
                p.do("T", (lambda bb=bb, k=k: lambda t: t.transpose(
                    psbf[:, bb * 1024 + k * 128: bb * 1024 + (k + 1) * 128],
                    lnt[:, bb, k * 128:(k + 1) * 128], identbf[:, :]))(),
                    reads=[rLnt[bb], rConst], writes=[rPB[bb]], tag="f_tr")
            p.do("S", (lambda bb=bb: lambda s: s.activation(
                lnTt[:, bb, :, :], psbf[:, bb * 1024:(bb + 1) * 1024]
                .rearrange("pp (k s) -> pp k s", k=KC), AF.Copy))(),
                reads=[rPB[bb]], writes=[rLnT[bb]], tag="f_ev")

        def mms(tt):
            bb = tt % 2
            yb = bb * 2
            for k in range(KC):
                for hh in range(2):
                    p.do("T", (lambda bb=bb, k=k, hh=hh, yb=yb: lambda t: t.matmul(
                        bank(yb + hh)[:, :], lnTt[:, bb, k, :],
                        wft[:, k, hh * 512:(hh + 1) * 512],
                        start=(k == 0), stop=(k == KC - 1)))(),
                        reads=[rLnT[bb], rWft[hh]], writes=[rPS[yb + hh]], tag="f_mm")
            for hh in range(2):
                p.do("V", (lambda tt=tt, hh=hh, yb=yb: lambda v: v.scalar_tensor_tensor(
                    X[:, tt, hh * 512:(hh + 1) * 512], bank(yb + hh)[:, :], 0.5,
                    X[:, tt, hh * 512:(hh + 1) * 512], op0=ALU.mult, op1=ALU.add))(),
                    reads=[rPS[yb + hh], rX[tt]], writes=[rX[tt]], tag="f_res")

        for tt in range(T):
            prep(tt)
            if tt >= 1:
                mms(tt - 1)
            if post_tile and tt >= 2:
                post_tile(tt - 2)
        mms(T - 1)
        if post_tile:
            post_tile(T - 2)
            post_tile(T - 1)

    lnt = alloc(stack, "lnt", [128, 4, D], BF16, side="right")
    rLnt = [Res(), Res(), Res(), Res()]
    aw_stack = contextlib.ExitStack()
    wq_sb = alloc(aw_stack, "wq_sb", [128, KC, H * DH], BF16, side="right")
    wk_sb = alloc(aw_stack, "wk_sb", [128, KC, H * DH], BF16, side="right")
    wvg_sb = alloc(aw_stack, "wvg_sb", [128, KC, H * DH + H], BF16, side="right")
    wfl = alloc(aw_stack, "wfl", [128, 4, H * DH], F32, side="right")
    rWq, rWk, rWvg = Res(), Res(), Res()
    rWfl = [Res() for _ in range(4)]

    f1_stack = contextlib.ExitStack()
    wft = alloc(f1_stack, "wft1", [128, KC, D], BF16, side="right")
    lnTt = alloc(f1_stack, "lnT1", [128, 2, KC, 128], BF16, side="right")
    wld = alloc(f1_stack, "wld1", [128, 2, D], F32, side="right")
    wsg = alloc(f1_stack, "wsg1", [128, D], BF16, side="right")
    rWft, rWld, rWsg, rLnT = [Res(), Res()], [Res(), Res()], Res(), [Res(), Res()]
    wftsp_h = nc.dram_tensor("wft_sp", [128, KC * D], BF16, kind="Internal").ap()

    build_wft(wft, rWft, wld, rWld, wsg, rWsg, "1")
    ln_stats(0)
    nwl = [0]
    for k in range(KC):
        for dst, src, rdst in ((wq_sb, wq_h, rWq), (wk_sb, wk_h, rWk), (wvg_sb, wv_h, rWvg)):
            fb = nwl[0] % 4
            nwl[0] += 1
            p.do("Q", (lambda k=k, src=src, fb=fb: lambda g: g.dma_start(
                out=wfl[:, fb, :].rearrange("pp (h e) -> pp h e", h=H),
                in_=src[:, k * 128:(k + 1) * 128, :].rearrange("h d e -> d h e")))(),
                writes=[rWfl[fb]], dma=True, tag="ldw_a")
            p.do("V", (lambda k=k, dst=dst, fb=fb: lambda v: v.tensor_scalar(
                dst[:, k, 0:H * DH], wfl[:, fb, :], rmswc[:, k:k + 1], None,
                op0=ALU.mult))(),
                reads=[rWfl[fb], rRmsw], writes=[rdst], tag="sclw_a")
        fb = nwl[0] % 4
        nwl[0] += 1
        p.do("Q", (lambda k=k, fb=fb: lambda g: g.dma_start(
            out=wfl[:, fb, 0:H], in_=wg_h[k * 128:(k + 1) * 128, :]))(),
            writes=[rWfl[fb]], dma=True, tag="ldwg")
        p.do("V", (lambda k=k, fb=fb: lambda v: v.tensor_scalar(
            wvg_sb[:, k, H * DH:H * DH + H], wfl[:, fb, 0:H], rmswc[:, k:k + 1], None,
            op0=ALU.mult))(),
            reads=[rWfl[fb], rRmsw], writes=[rWvg], tag="sclwg")

    def _a0_post(tt):
        stats_tile(8 * T, tt)
        if tt == T // 2 - 1:
            stats_math(8 * T, rms=True, lo=0, hi=T // 2)
        if tt == T - 1:
            stats_math(8 * T, rms=True, lo=T // 2, hi=T)
    ffn(0, wft, rWft, lnt, rLnt, lnTt, rLnT, post_tile=_a0_post)
    p.do("Q", lambda q: q.dma_start(
        out=wftsp_h, in_=wft.ap().rearrange("pp k dd -> pp (k dd)")),
        reads=[rWft[0], rWft[1]], dma=True, tag="wft_sp")
    f1_stack.close()
    p.fence()

    qhat = alloc(stack, "qhat", [65, H, N], BF16)
    khat = alloc(stack, "khat", [65, H, N], BF16)
    Vsb = alloc(stack, "Vsb", [128, T, H, DH + 1], BF16)
    g_tok = alloc(stack, "g_tok", [128, T, H], BF16)
    gscr = alloc(stack, "gscr", [4, 2, 128], BF16)
    p.do("V", lambda v: v.memset(
        Vsb[:, :, :, DH:DH + 1], 1.0),
        writes=[rV[t] for t in range(T)], tag="vones")
    p.do("V", lambda v: v.memset(khat[64:65, :, :], -1.0),
         writes=[rKhat[h] for h in range(H)], tag="khat1")

    a12_stack = contextlib.ExitStack()
    tTc = alloc(a12_stack, "tTc", [128, KC, 2, 512], BF16, side="right")
    rTTc = [Res(), Res()]


    def emit_tprep(sc):
        cb = sc % 2
        for i4 in range(4):
            tt = sc * 4 + i4
            bb = tt % 4
            p.do("S", (lambda tt=tt, bb=bb: lambda s: s.activation(
                lnt[:, bb, :], X[:, tt, :], AF.Identity,
                bias=stat_col(8 * T, 3, tt), scale=stat_col(8 * T, 2, tt)))(),
                reads=[rX[tt], rSt(8 * T, tt // (T // 2))],
                writes=[rLnt[bb]], tag="t_ln")
        for i4 in range(4):
            tt = sc * 4 + i4
            bb = tt % 4
            hb = tt % 2
            for k in range(KC):
                p.do("T", (lambda bb=bb, hb=hb, k=k: lambda t: t.transpose(
                    psbf[:, hb * 1024 + k * 128: hb * 1024 + (k + 1) * 128],
                    lnt[:, bb, k * 128:(k + 1) * 128], identbf[:, :]))(),
                    reads=[rLnt[bb], rConst], writes=[rPB[hb]], tag="t_tr")
            p.do("V", (lambda cb=cb, hb=hb, i4=i4: lambda v: v.tensor_copy(
                tTc[:, :, cb, i4 * 128:(i4 + 1) * 128],
                psbf[:, hb * 1024:(hb + 1) * 1024]
                .rearrange("pp (k s) -> pp k s", k=KC)))(),
                reads=[rPB[hb]], writes=[rTTc[cb]], tag="t_ev")

    def emit_qkvg(sc):
        cb = sc % 2
        for h in range(H):
            bq = (h % 2) * 2
            bk = bq + 1
            for k in range(KC):
                p.do("T", (lambda h=h, k=k, bq=bq, cb=cb: lambda t: t.matmul(
                    bank(bq)[0:64, :], wq_sb[:, k, h * DH:(h + 1) * DH],
                    tTc[:, k, cb, :], start=(k == 0), stop=(k == KC - 1)))(),
                    reads=[rWq, rTTc[cb]], writes=[rPS[bq]], tag="qk_q")
                p.do("T", (lambda h=h, k=k, bk=bk, cb=cb: lambda t: t.matmul(
                    bank(bk)[0:64, :], wk_sb[:, k, h * DH:(h + 1) * DH],
                    tTc[:, k, cb, :], start=(k == 0), stop=(k == KC - 1)))(),
                    reads=[rWk, rTTc[cb]], writes=[rPS[bk]], tag="qk_k")
            p.do("V", (lambda h=h, sc=sc, bq=bq: lambda v: v.tensor_scalar_mul(
                qhat[0:64, h, sc * 512:(sc + 1) * 512], bank(bq)[0:64, :], 0.125))(),
                reads=[rPS[bq]], writes=[rQhat[h]], tag="qk_evq")
            p.do("V", (lambda h=h, sc=sc, bk=bk: lambda v: v.tensor_copy(
                khat[0:64, h, sc * 512:(sc + 1) * 512], bank(bk)[0:64, :]))(),
                reads=[rPS[bk]], writes=[rKhat[h]], tag="qk_evk")
        for i4 in range(4):
            tt = sc * 4 + i4
            bv = 4 + (i4 % 2)
            for k in range(KC):
                p.do("T", (lambda k=k, bv=bv, cb=cb, i4=i4: lambda t: t.matmul(
                    bank(bv)[:, :], tTc[:, k, cb, i4 * 128:(i4 + 1) * 128],
                    wvg_sb[:, k, 0:H * DH], start=(k == 0), stop=(k == KC - 1)))(),
                    reads=[rTTc[cb], rWvg], writes=[rPS[bv]], tag="v_mm")
                p.do("T", (lambda k=k, cb=cb, i4=i4: lambda t: t.matmul(
                    bank(0)[:, 0:H], tTc[:, k, cb, i4 * 128:(i4 + 1) * 128],
                    wvg_sb[:, k, H * DH:H * DH + H], start=(k == 0), stop=(k == KC - 1)))(),
                    reads=[rTTc[cb], rWvg], writes=[rPS[0]], tag="g_mm")
            p.do("V", (lambda tt=tt, bv=bv: lambda v: v.tensor_copy(
                Vsb[:, tt, :, 0:DH],
                bank(bv).rearrange("pp (h e) -> pp h e", h=H)))(),
                reads=[rPS[bv]], writes=[rV[tt]], tag="v_ev")
            p.do("S", (lambda tt=tt: lambda s: s.activation(
                g_tok[:, tt, :], bank(0)[:, 0:H], AF.Sigmoid))(),
                reads=[rPS[0]], writes=[rGtok], tag="g_ev")

    for sc in range(IC):
        emit_tprep(sc)
        if sc >= 1:
            emit_qkvg(sc - 1)
    emit_qkvg(IC - 1)
    a12_stack.close()
    aw_stack.close()
    p.fence(exclude=("V",))

    a3_stack = contextlib.ExitStack()
    wo_bf = nc.dram_tensor("wo_bf", [H, DH, D], BF16, kind="Internal").ap()
    wobuf = alloc(a3_stack, "wobuf", [64, 4, D], BF16, side="right")
    outev = alloc(a3_stack, "outev", [64, 2, 512], F32, side="right")
    outs = alloc(a3_stack, "outs", [64, H, 512], BF16, side="right")
    Lrow = alloc(a3_stack, "Lrow", [65, 2, 512], F32, side="right")
    gRow = alloc(a3_stack, "gRow", [65, 2, 512], BF16, side="right")
    c2row = alloc(a3_stack, "c2row", [65, 2, 512], F32, side="right")
    c2bf = alloc(a3_stack, "c2bf", [65, 2, 512], BF16, side="right")
    c2full = alloc(a3_stack, "c2full", [64, 2, 512], BF16, side="right")
    ones64 = alloc(a3_stack, "ones64", [65, 64], BF16, side="right")
    lnmk = alloc(a3_stack, "lnmk", [1, H], F32, side="right")
    bnd_stack = contextlib.ExitStack()
    qks = alloc(bnd_stack, "qks", [64, 2, N], BF16, side="right")
    rowA = alloc(bnd_stack, "rowA", [1, 512], F32, side="right")
    pmk = alloc(bnd_stack, "pmk", [1, 8], F32, side="right")
    rWo = Res()
    rWobuf = [Res() for _ in range(4)]
    rPt = [Res() for _ in range(4)]
    rC2 = Res()
    rC2full = [Res(), Res()]
    p.do("V", lambda v: v.memset(ones64[:, :], 1.0), writes=[rConst], tag="ones64")
    rOutev = [Res(), Res()]
    rOuts = Res()
    rLrow = [Res(), Res()]
    rGrow = [Res(), Res()]
    rQks = Res()
    rRowA = Res()
    rPmk = Res()
    rLnmk = Res()

    p.do("G", lambda g: g.dma_start(
        out=wo_bf.rearrange("h e d -> (h e) d"),
        in_=wo_h.rearrange("h e d -> (h e) d")),
        writes=[rWo], dma=True, tag="wo_cast")

    ringc = [0]

    def ring1():
        b = ringc[0] % 4
        ringc[0] += 1
        return b

    def ring2():
        if ringc[0] % 2:
            ringc[0] += 1
        b = ringc[0] % 4
        ringc[0] += 2
        return b

    def emit_bound(h):
        ext = ([rTTc[0], rTTc[1], rWq, rWk, rWvg] + rWfl) if h == 0 else []
        p.do("V", (lambda h=h: lambda v: v.tensor_mul(
            qks[:, 0, :], qhat[0:64, h, :], qhat[0:64, h, :]))(),
            reads=[rQhat[h]] + ext, writes=[rQks], tag="b_qsq")
        p.do("V", (lambda h=h: lambda v: v.tensor_mul(
            qks[:, 1, :], khat[0:64, h, :], khat[0:64, h, :]))(),
            reads=[rKhat[h]], writes=[rQks], tag="b_ksq")
        kslots = []
        for icc in range(IC):
            b = ring1()
            kslots.append(b)
            p.do("T", (lambda icc=icc, b=b: lambda t: t.matmul(
                bank(b)[0:1, :], onesc[0:64, 0:1], qks[:, 1, icc * 512:(icc + 1) * 512],
                start=True, stop=True))(),
                reads=[rQks, rConst], writes=[rPS[b]], tag="b_kmm")
        for icc, b in enumerate(kslots):
            p.do("V", (lambda icc=icc, b=b: lambda v: v.reduce_max(
                pmk[0:1, icc:icc + 1], bank(b)[0:1, :], axis=AX.X))(),
                reads=[rPS[b]], writes=[rPmk], tag="b_kpm")
        p.do("V", (lambda: lambda v: v.reduce_max(
            pmk[0:1, IC:IC + 1], pmk[0:1, 0:IC], axis=AX.X))(),
            reads=[rPmk], writes=[rPmk], tag="b_kmx")
        p.do("S", (lambda h=h: lambda s: s.activation(
            pmk[0:1, IC + 1:IC + 2], pmk[0:1, IC:IC + 1], AF.Ln, bias=1e-30))(),
            reads=[rPmk], writes=[rPmk], tag="b_lnk")
        p.do("V", (lambda h=h: lambda v: v.tensor_scalar_mul(
            lnmk[0:1, h:h + 1], pmk[0:1, IC + 1:IC + 2], 0.5))(),
            reads=[rPmk], writes=[rLnmk], tag="b_lnk2")
        for icc in range(IC):
            b = ring1()
            p.do("T", (lambda icc=icc, b=b: lambda t: t.matmul(
                bank(b)[0:1, :], onesc[0:64, 0:1], qks[:, 0, icc * 512:(icc + 1) * 512],
                start=True, stop=True))(),
                reads=[rQks, rConst], writes=[rPS[b]], tag="b_qmm")
            p.do("S", (lambda icc=icc, b=b: lambda s: s.activation(
                rowA[0:1, :], bank(b)[0:1, :], AF.Ln, bias=1e-30))(),
                reads=[rPS[b]], writes=[rRowA], tag="b_lnq")
            p.do("S", (lambda h=h, icc=icc: lambda s: s.activation(
                qhat[64:65, h, icc * 512:(icc + 1) * 512], rowA[0:1, :], AF.Exp,
                scale=0.5, bias=lnmk[0:1, h:h + 1]))(),
                reads=[rRowA, rLnmk], writes=[rQhat[h]], tag="b_c")

    pending_gate = [None]
    pending_gate_t = [None]
    pend_pv = [None]
    pend_epi = [None]

    def flush_gate_t():
        if pending_gate_t[0] is not None:
            pending_gate_t[0]()
            pending_gate_t[0] = None

    def flush_gate():
        flush_gate_t()
        if pending_gate[0] is not None:
            pending_gate[0]()
            pending_gate[0] = None

    def flush_tail():
        if pend_pv[0] is not None:
            pend_pv[0]()
            pend_pv[0] = None
        if pend_epi[0] is not None:
            pend_epi[0]()
            pend_epi[0] = None

    def emit_pass2(ic, h):
        ob = h % 2
        ab = 4 + (h % 2)

        def emit_grow(h=h, ic=ic, ob=ob):
            gp = h % 2
            gof = gp * 1024
            p.do("T", (lambda h=h, ic=ic, gof=gof: lambda t: t.transpose(
                psbf[0:4, gof:gof + 128], g_tok[:, 4 * ic:4 * ic + 4, h],
                identbf[:, :]))(),
                reads=[rGtok, rConst], writes=[rPB[gp]], tag="g_tr")
            p.do("V", (lambda gp=gp, gof=gof: lambda v: v.tensor_copy(
                gscr[:, gp, :], psbf[0:4, gof:gof + 128]))(),
                reads=[rPB[gp]], writes=[rGscr], tag="g_scr")
            p.do("Q", (lambda ob=ob, gp=gp: lambda q: q.dma_start(
                out=gRow[64:65, ob, :].rearrange("o (t f) -> o t f", t=4),
                in_=gscr[:, gp, :]))(),
                reads=[rGscr], writes=[rGrow[ob]], dma=True, chain=0, tag="g_dma")
        npair = 2 * (ic + 1)

        def emit_pv(pg, pb):
            for b in range(2):
                jt = 2 * pg + b
                p.do("T", (lambda h=h, jt=jt, b=b, pb=pb, ab=ab,
                           first=(pg == 0 and b == 0),
                           last=(pg == npair - 1 and b == 1): lambda t: t.matmul(
                    bank(ab)[0:65, :], Vsb[:, jt, h, :],
                    pt[:, pb, b * 512:(b + 1) * 512],
                    start=first, stop=last))(),
                    reads=[rPt[pb], rV[jt]], writes=[rPS[ab]], tag="pv_mm")

        def epilogue(h=h, ob=ob, ab=ab):
            p.do("V", (lambda ob=ob, ab=ab: lambda v: v.tensor_copy(
                outev[:, ob, :], bank(ab)[0:64, :]))(),
                reads=[rPS[ab]], writes=[rOutev[ob]], tag="pv_ev")
            if h == H - 1:
                flush_gate()
                p.do("S", (lambda ob=ob, ab=ab: lambda s: s.activation(
                    c2row[64:65, ob, :], bank(ab)[64:65, :], AF.Ln))(),
                    reads=[rPS[ab]], writes=[rC2], tag="c2ln_f")
                p.do("S", (lambda ob=ob: lambda s: s.activation(
                    c2row[64:65, ob, :], c2row[64:65, ob, :], AF.Exp, scale=-1.0))(),
                    reads=[rC2], writes=[rC2], tag="c2exp_f")
                p.do("V", (lambda ob=ob: lambda v: v.tensor_mul(
                    c2bf[64:65, ob, :], c2row[64:65, ob, :], gRow[64:65, ob, :]))(),
                    reads=[rGrow[ob]], writes=[rC2], tag="c2_f")
                cb_b = ring1()
                p.do("T", (lambda ob=ob, cb_b=cb_b: lambda t: t.matmul(
                    bank(cb_b)[0:64, :], ones64[64:65, 0:64], c2bf[64:65, ob, :],
                    start=True, stop=True))(),
                    reads=[rC2, rConst], writes=[rPS[cb_b]], tag="cb_mm_f")
                p.do("V", (lambda h=h, ob=ob, cb_b=cb_b: lambda v: v.tensor_tensor(
                    outs[:, h, :], outev[:, ob, :], bank(cb_b)[0:64, :],
                    op=ALU.mult))(),
                    reads=[rOutev[ob], rPS[cb_b]], writes=[rOuts], tag="c_scale_f")
                pending_gate[0] = None
                return
            p.do("V", (lambda ob=ob, ab=ab: lambda v: v.tensor_copy(
                Lrow[64:65, ob, :], bank(ab)[64:65, :]))(),
                reads=[rPS[ab]], writes=[rLrow[ob]], tag="l_ev")

            def gate(h=h, ob=ob):
                p.do("S", (lambda ob=ob: lambda s: s.activation(
                    c2row[64:65, ob, :], Lrow[64:65, ob, :], AF.Ln))(),
                    reads=[rLrow[ob]], writes=[rC2], tag="c2ln")
                p.do("S", (lambda ob=ob: lambda s: s.activation(
                    c2row[64:65, ob, :], c2row[64:65, ob, :], AF.Exp, scale=-1.0))(),
                    reads=[rC2], writes=[rC2], tag="c2exp")
                p.do("V", (lambda ob=ob: lambda v: v.tensor_mul(
                    c2bf[64:65, ob, :], c2row[64:65, ob, :], gRow[64:65, ob, :]))(),
                    reads=[rGrow[ob]], writes=[rC2], tag="c2")

                def _bc(ob=ob):
                    def run(q):
                        s = c2bf[64:65, ob, :]
                        bsrc = AP(s.tensor, s.offset,
                                  [(s.ap[0][0], 1), (0, 64)] + list(s.ap[1:]))
                        return q.dma_start(out=c2full[:, ob, :], in_=bsrc)
                    return run
                p.do("Q", _bc(), reads=[rC2], writes=[rC2full[ob]],
                     dma=True, chain=1, tag="c2_bc")

                def gate_t(h=h, ob=ob):
                    p.do("V", (lambda h=h, ob=ob: lambda v: v.tensor_tensor(
                        outs[:, h, :], outev[:, ob, :], c2full[:, ob, :],
                        op=ALU.mult))(),
                        reads=[rOutev[ob], rC2full[ob]], writes=[rOuts], tag="c_scale")
                pending_gate_t[0] = gate_t
            flush_gate()
            pending_gate[0] = gate

        lpv = []
        for pg in range(npair):
            pb = pg % 4
            if pg == 1:
                flush_tail()
                emit_grow()
            b0 = ring2()
            for b in range(2):
                jt = 2 * pg + b
                p.do("T", (lambda h=h, ic=ic, jt=jt, b0=b0, b=b:
                           lambda t: t.matmul(
                    bank(b0 + b)[:, :], khat[:, h, jt * 128:(jt + 1) * 128],
                    qhat[:, h, ic * 512:(ic + 1) * 512],
                    start=True, stop=True))(),
                    reads=[rQhat[h], rKhat[h]], writes=[rPS[b0 + b]], tag="s2_mm")
            p.do("S", (lambda pb=pb, b0=b0: lambda s: s.activation(
                pt[:, pb, :], banks(b0, 2), AF.Exp))(),
                reads=[rPS[b0], rPS[b0 + 1]], writes=[rPt[pb]], tag="s2_exp")
            for b in range(2):
                jt = 2 * pg + b
                r = jt - 4 * ic
                if r >= 0:
                    if r > 0:
                        p.do("V", (lambda pb=pb, b=b, r=r: lambda v: v.memset(
                            pt[:, pb, b * 512: b * 512 + r * 128], 0.0))(),
                            writes=[rPt[pb]], tag="z_slab")
                    p.do("V", (lambda pb=pb, b=b, r=r: lambda v: v.tensor_mul(
                        pt[:, pb, b * 512 + r * 128: b * 512 + (r + 1) * 128],
                        pt[:, pb, b * 512 + r * 128: b * 512 + (r + 1) * 128],
                        ubin[:, :]))(),
                        reads=[rConst], writes=[rPt[pb]], tag="z_tri")
            if pg == 0:
                flush_gate_t()
            lpv.append((pg, pb))
            if len(lpv) > 2:
                emit_pv(*lpv.pop(0))
        if len(lpv) > 1:
            emit_pv(*lpv.pop(0))
        pend_pv[0] = (lambda a=lpv.pop(0): lambda: emit_pv(*a))()
        pend_epi[0] = epilogue

    def emit_wo(ic):
        flush_gate_t()
        flush_tail()
        flush_gate()
        flush_gate_t()
        for ph in range(2):
            for h in range(H):
                wb = h % 4
                p.do("Q", (lambda h=h, wb=wb: lambda q: q.dma_start(
                    out=wobuf[:, wb, :], in_=wo_bf[h, :, :]))(),
                    reads=[rWo], writes=[rWobuf[wb]], dma=True, chain=2 + (h % 2),
                    tag="ldwo")
                for i2 in range(2):
                    pb0 = i2 * 2
                    itr = ph * 2 + i2
                    for hh in range(2):
                        p.do("T", (lambda h=h, wb=wb, itr=itr, hh=hh, pb0=pb0:
                                   lambda t: t.matmul(
                            bank(pb0 + hh)[:, :], outs[:, h, itr * 128:(itr + 1) * 128],
                            wobuf[:, wb, hh * 512:(hh + 1) * 512],
                            start=(h == 0), stop=(h == H - 1)))(),
                            reads=[rOuts, rWobuf[wb]], writes=[rPS[pb0 + hh]], tag="wo_mm")
            for i2 in range(2):
                itr = ph * 2 + i2
                tt = 4 * ic + itr
                pb0 = i2 * 2
                for hh in range(2):
                    p.do("V", (lambda tt=tt, hh=hh, pb0=pb0: lambda v: v.tensor_add(
                        X[:, tt, hh * 512:(hh + 1) * 512],
                        X[:, tt, hh * 512:(hh + 1) * 512], bank(pb0 + hh)[:, :]))(),
                        reads=[rPS[pb0 + hh], rX[tt]], writes=[rX[tt]], tag="wo_res")
                stats_tile(16 * T, tt)

    for h in range(H):
        emit_bound(h)
    bnd_stack.close()
    pt = alloc(a3_stack, "pt", [128, 4, 1024], BF16, side="right")
    for ic in range(IC):
        for h in range(H):
            emit_pass2(ic, h)
        emit_wo(ic)
    a3_stack.close()
    p.fence()

    stats_math(16 * T, lo=0, hi=T // 2)
    stats_math(16 * T, lo=T // 2, hi=T)

    f2_stack = contextlib.ExitStack()
    wft2 = alloc(f2_stack, "wft2", [128, KC, D], BF16, side="right")
    lnTt2 = alloc(f2_stack, "lnT2", [128, 2, KC, 128], BF16, side="right")
    rWft2, rLnT2 = [Res(), Res()], [Res(), Res()]
    p.do("Q", lambda q: q.dma_start(
        out=wft2.ap().rearrange("pp k dd -> pp (k dd)"), in_=wftsp_h),
        writes=[rWft2[0], rWft2[1]], dma=True, tag="wft_rl")
    ffn(16 * T, wft2, rWft2, lnt, rLnt, lnTt2, rLnT2)

    for tt in range(T):
        p.do("Q", (lambda tt=tt: lambda q: q.dma_start(
            out=out_h.rearrange("(t pp) d -> pp t d", pp=128)[:, tt, :],
            in_=X[:, tt, :]))(),
            reads=[rX[tt]], dma=True, tag="stX")
    f2_stack.close()

    totals = p.finalize()

    sem_keys = ["T", "V", "S"] + [f"{e}{c}" for e in ("G", "Q") for c in range(Prog.NCHAIN)]
    sems = {k: stack.enter_context(nc.semaphore("sem" + k)) for k in sem_keys}

    with nc.Block() as block:
        def make_body(ename):
            def body(eng):
                for op in p.ops[ename]:
                    if ename not in ("G", "Q") and op.drain_before:
                        eng.drain()
                    for pe, v in op.waits:
                        eng.wait_ge(sems[pe], v)
                    inst = op.emit(eng)
                    if op.need_inc:
                        inst.then_inc(sems[op.sem_key], 16 if op.dma else 1)
                if ename == "Q":
                    for c in range(Prog.NCHAIN):
                        if totals.get(f"Q{c}", 0):
                            eng.wait_ge(sems[f"Q{c}"], totals[f"Q{c}"])
                        if totals.get(f"G{c}", 0):
                            eng.wait_ge(sems[f"G{c}"], totals[f"G{c}"])
            return body
        block.tensor(make_body("T"))
        block.vector(make_body("V"))
        block.scalar(make_body("S"))
        block.gpsimd(make_body("G"))
        block.sync(make_body("Q"))

    stack.close()
    return nc


def _consts():
    ident = np.eye(128)
    f = np.arange(128)[None, :]
    pp = np.arange(128)[:, None]
    consts = {
        "identbf": ident.astype(BF),
        "identf": ident.astype(np.float32),
        "umask": np.where(f > pp, NEGBIG, 0.0).astype(BF),
        "ubin": (f >= pp).astype(np.float32).astype(BF),
    }
    return consts


def kernel(L0_attn_tokens, L0_attn_wq, L0_attn_wk, L0_attn_wv, L0_attn_wo,
           L0_attn_w_lr, L0_attn_w_target, L0_attn_w_gates, L0_attn_rms_w,
           L0_ffn_weight):
    B, N, D_ = L0_attn_tokens.shape
    assert D_ == D
    nc = build_program(N)
    common = {
        "wq": np.ascontiguousarray(L0_attn_wq, np.float32),
        "wk": np.ascontiguousarray(L0_attn_wk, np.float32),
        "wv": np.ascontiguousarray(L0_attn_wv, np.float32),
        "wo": np.ascontiguousarray(L0_attn_wo, np.float32),
        "wg": np.ascontiguousarray(L0_attn_w_gates, np.float32),
        "rmsw": np.ascontiguousarray(np.asarray(L0_attn_rms_w, np.float32).reshape(KC, 128).T),
        "wf": np.ascontiguousarray(L0_ffn_weight, np.float32),
    }
    common.update(_consts())
    in_maps = []
    for c in range(B):
        m = dict(common)
        m["x"] = np.ascontiguousarray(L0_attn_tokens[c], np.float32)
        in_maps.append(m)
    res = run_bass_kernel_spmd(nc, in_maps, core_ids=list(range(B)))
    return np.stack([res.results[c]["out"] for c in range(B)], axis=0)
